# version 29
# baseline (speedup 1.0000x reference)
from contextlib import ExitStack

import numpy as np
import ml_dtypes

import concourse.bass as bass
import concourse.mybir as mybir
from concourse.bass_utils import run_bass_kernel_spmd

F32 = mybir.dt.float32
BF16 = mybir.dt.bfloat16
I32 = mybir.dt.int32
AX = mybir.AxisListType.X
AL = mybir.AluOpType
AF = mybir.ActivationFunctionType

B, S, D = 4, 4096, 1024
DC = D // 128
I_FFN = 2816
IC = I_FFN // 128
T = 128
HALO = 256
CHUNK = 32
NEG = -1.0e30
C_RHI = float(np.float32(1.5 * 2 ** 23))
SIGNBIT = 0x80000000 - 2 ** 32

R_K = 6
R_PK = 5
R_V = 7
R_PV = 6
XR = 2

SC_W = 640

GUARDS = False
G_WRAP = True
G_DRAIN = True
G_DMAW = False
Q_SPLIT = 1.0
G_DRAIN_ELEM = False
PORTABLE = True

ENGINES = ("sync", "gpsimd", "vector", "scalar", "tensor")


def _runs(tiles, ring):
    out = []
    start = 0
    for k in range(1, len(tiles) + 1):
        if k == len(tiles) or tiles[k] % ring != (tiles[k - 1] % ring) + 1:
            out.append((tiles[start] % ring, start, k - start))
            start = k
    return out


class Prog:

    def __init__(self, nc, stack):
        self.nc = nc
        self.stack = stack
        self.ops = {e: [] for e in ENGINES}
        self.lastop = {e: None for e in ENGINES}
        self.sems = {}

    def sem(self, name):
        h = self.stack.enter_context(self.nc.semaphore(name))
        self.sems[name] = [h, 0]

    def inc(self, name, k=1):
        s = self.sems[name]
        s[1] += k
        return s[1]

    def val(self, name):
        return self.sems[name][1]

    def h(self, name):
        return self.sems[name][0]

    def on(self, eng, fn, is_op=True):
        self.ops[eng].append(fn)
        self.lastop[eng] = fn if is_op else None

    def wait(self, eng, name, v=None):
        if v is None:
            v = self.val(name)
        if v <= 0:
            return
        h = self.h(name)
        self.on(eng, lambda e, h=h, v=v: e.wait_ge(h, v), is_op=False)

    def dma(self, eng, dst, src, sem):
        h = self.h(sem)
        if GUARDS or G_DMAW:
            v = self.val(sem)
            if v > 0:
                self.on(eng, lambda e, h=h, v=v: e.wait_ge(h, v), is_op=False)
        self.inc(sem, 16)
        self.on(eng, lambda e, d=dst, s=src, h=h: e.dma_start(d, s).then_inc(h, 16),
                is_op=False)

    def barrier_inc(self, eng, name, k=1):
        h = self.h(name)
        self.inc(name, k)
        if not GUARDS and self.lastop[eng] is not None and                 self.ops[eng] and self.ops[eng][-1] is self.lastop[eng]:
            prev = self.ops[eng].pop()
            self.ops[eng].append(
                lambda e, p=prev, h=h, k=k: p(e).then_inc(h, k))
            self.lastop[eng] = None
        else:
            def f(e, h=h, k=k):
                e.drain()
                e.sem_inc(h, k)
            self.on(eng, f, is_op=False)

    def drain(self, eng, hard=True):
        if GUARDS or (G_DRAIN and eng != "gpsimd" and (hard or G_DRAIN_ELEM)):
            self.on(eng, lambda e: e.drain(), is_op=False)

    def pe_inc(self, name, k=1):
        self.barrier_inc("tensor", name, k)
        return self.val(name)

    def phase_barrier(self, tag, dma_waits=()):
        name = "s_ph_" + tag
        self.sem(name)
        for eng, semn in dma_waits:
            self.wait(eng, semn)
        for eng in ENGINES:
            self.barrier_inc(eng, name)
        v = self.val(name)
        for eng in ENGINES:
            self.wait(eng, name, v)


def make_host_constants(typeA):
    dt_b = ml_dtypes.bfloat16
    am = np.zeros((128, 3, 384), np.float32)
    dlt = np.arange(128)[:, None]
    j = np.arange(384)[None, :]
    band = (j >= dlt) & (j <= dlt + 255)
    for v in range(3):
        valid = band.copy()
        if typeA and v == 0:
            valid &= (j >= 256)
        if typeA and v == 1:
            valid &= (j >= 128)
        am[:, v, :] = np.where(valid, 0.0, NEG)
    amask = am.reshape(128, 3 * 384).astype(dt_b)

    mc = np.zeros((128, SC_W), np.float32)
    mc[:, 0:384][band] = 1.0
    for d_i in range(128):
        mc[d_i, d_i + 255] = 2.0
    mc[0, 384 + 127] = 1.0
    for d_i in range(1, 128):
        mc[d_i, 512 + d_i - 1] = 1.0
    maskc = mc.astype(dt_b)

    ident = np.eye(128, dtype=np.float32)
    identb = np.eye(128).astype(dt_b)
    a4t = np.zeros((128, 128), np.float32)
    for i in range(128):
        for jj in range(128):
            if i // CHUNK == jj // CHUNK and i % CHUNK <= jj % CHUNK:
                a4t[i, jj] = 1.0
    krow = np.tile((np.arange(512) % CHUNK + 1).astype(np.float32)[None, :],
                   (128, 1))
    ones1 = np.ones((1, 128), dt_b)
    sgnbit = np.full((128, 1), -2147483648, np.int32)
    absmask = np.full((128, 1), 0x7FFFFFFF, np.int32)
    epscol = np.full((128, 1), 1e-5, np.float32)
    rowmask = np.ones((128, 1), np.float32)
    if typeA:
        rowmask[0, 0] = 0.0
    return dict(amask=amask, maskc=maskc, ident=ident, identb=identb,
                a4t=a4t, krow=krow, ones1=ones1, epscol=epscol,
                rowmask=rowmask, sgnbit=sgnbit, absmask=absmask)


def build_nc(n_own, debug=False):
    n_ext = n_own + HALO
    NT_OWN = n_own // T
    NT_EXT = n_ext // T
    sweeps = []
    t0 = 0
    while t0 < NT_EXT:
        sweeps.append(list(range(t0, min(t0 + 4, NT_EXT))))
        t0 += 4

    nc = bass.Bass("TRN2", target_bir_lowering=False,
                   dynamic_dma_scratch_size=4096,
                   detect_race_conditions=GUARDS)
    global GUARDS_USED
    GUARDS_USED = GUARDS

    def din(name, shape, dtype=F32):
        return nc.dram_tensor(name, list(shape), dtype, kind="ExternalInput").ap()

    x_ext_d = din("x_ext", (n_ext, D))
    wT_d = {w: din("w" + w + "T", (D, D)) for w in ("q", "k", "v", "pk", "pv")}
    bq_d = din("bq", (128, DC))
    bk_d = din("bk", (128, DC))
    bpk_d = din("bpk", (128, DC))
    vb_d = din("vb", (1, D))
    pvb_d = din("pvb", (1, D))
    wg_d = din("wg", (I_FFN, D))
    wu_d = din("wu", (I_FFN, D))
    wd_d = din("wd", (D, I_FFN))
    amask_d = din("amask", (128, 3 * 384), BF16)
    maskc_d = din("maskc", (128, SC_W), BF16)
    ident_d = din("ident", (128, 128))
    identb_d = din("identb", (128, 128), BF16)
    a4t_d = din("a4t", (128, 128))
    krow_d = din("krow", (128, 512))
    ones1_d = din("ones1", (1, 128), BF16)
    sgnbit_d = din("sgnbit", (128, 1), I32)
    absmask_d = din("absmask", (128, 1), I32)
    epscol_d = din("epscol", (128, 1))
    rowmask_d = din("rowmask", (128, 1))
    out_d = nc.dram_tensor("out", [n_own, D], F32, kind="ExternalOutput").ap()
    attn_hbm = nc.dram_tensor("attn_hbm", [n_own, D], BF16, kind="Internal").ap()
    xn_hbm = nc.dram_tensor("xn_hbm", [n_own, D], F32, kind="Internal").ap()
    dbg_attn = (nc.dram_tensor("dbg_attn", [n_own, D], BF16,
                               kind="ExternalOutput").ap() if debug else None)
    dbg_small = (nc.dram_tensor("dbg_small", [128, 48], F32,
                                kind="ExternalOutput").ap() if debug else None)
    dbg_big = (nc.dram_tensor("dbg_big", [128, SC_W + 20], F32,
                              kind="ExternalOutput").ap() if debug else None)

    with ExitStack() as top:
        E = top.enter_context
        P = Prog(nc, top)

        identb = E(nc.sbuf_tensor("identb_s", [128, 128], BF16))
        ones1 = E(nc.sbuf_tensor("ones1_s", [1, 128], BF16))
        sgnbit = E(nc.sbuf_tensor("sgnbit_s", [128, 1], I32))
        absmask = E(nc.sbuf_tensor("absmask_s", [128, 1], I32))
        epscol = E(nc.sbuf_tensor("epscol_s", [128, 1], F32))
        bq_s = E(nc.sbuf_tensor("bq_s", [128, DC], F32))
        bk_s = E(nc.sbuf_tensor("bk_s", [128, DC], F32))
        bpk_s = E(nc.sbuf_tensor("bpk_s", [128, DC], F32))
        vb_s = E(nc.sbuf_tensor("vb_s", [1, D], BF16))
        pvb_s = E(nc.sbuf_tensor("pvb_s", [1, D], BF16))
        dbgS = E(nc.sbuf_tensor("dbgS", [128, 48], F32)) if debug else None
        dbgB = E(nc.sbuf_tensor("dbgB", [128, SC_W + 20], F32)) if debug else None

        for s in ("s_cin", "s_w", "s_x", "s_ln", "s_v2s", "s_s2v",
                  "s_pe_tr", "s_evtr", "s_pe_proj", "s_evproj",
                  "s_pe_sc", "s_evsc", "s_vq", "s_exp", "s_vP",
                  "s_pe_pv", "s_attn", "s_adma",
                  "s_qd", "s_qv", "s_qpe", "s_qev", "s_xnd",
                  "s_bx", "s_bv", "s_bs", "s_bpe", "s_bev", "s_out", "s_qs"):
            P.sem(s)

        for dst, src in ((identb, identb_d), (ones1, ones1_d),
                         (epscol, epscol_d),
                         (sgnbit, sgnbit_d), (absmask, absmask_d),
                         (bq_s, bq_d), (bk_s, bk_d), (bpk_s, bpk_d)):
            P.dma("sync", dst[:], src, "s_cin")

        P.wait("vector", "s_cin")
        P.on("vector", lambda e: e.tensor_scalar_mul(bq_s[:], bq_s[:], 1.0 / 32.0))
        P.barrier_inc("vector", "s_ln")
        LN0 = P.val("s_ln")

        def _x3(ap, f0, f1):
            return ap[:, f0:f1].rearrange("p (b i) -> p b i", i=16)

        def _bc(ap, b0, b1):
            return ap[:, b0:b1].unsqueeze(2).to_broadcast([128, b1 - b0, 16])

        def equant_stats(F, x_ap, bufs, v_waits=(), s_waits=(), rd_sem=None):
            t_b, shs_b = bufs["t"], bufs["shs"]
            bm_b, z12_b, r2_b = bufs["bm"], bufs["z12"], bufs["r2"]
            nb = F // 16
            for sm, vv in s_waits:
                P.wait("scalar", sm, vv)
            P.on("scalar", lambda e: e.activation(t_b[:, 0:F], x_ap, AF.Abs))
            P.on("scalar", lambda e: e.activation(shs_b[:, 0:F], x_ap, AF.Sign))
            P.barrier_inc("scalar", "s_qs")
            for sm, vv in v_waits:
                P.wait("vector", sm, vv)
            P.on("vector", lambda e: e.tensor_reduce(
                bm_b[:, 0:nb].unsqueeze(2), _x3(x_ap, 0, F), axis=AX,
                op=AL.max, apply_absolute_value=True))
            P.drain("vector")
            if rd_sem is not None:
                hh = P.h(rd_sem)
                P.inc(rd_sem, 1)
                P.on("vector", lambda e, hh=hh: e.sem_inc(hh, 1), is_op=False)
            P.on("vector", lambda e: e.tensor_scalar(
                z12_b[:, 0:nb], bm_b[:, 0:nb], 6e-12, 1.0 / 12.0,
                op0=AL.max, op1=AL.mult))
            P.drain("vector")
            P.on("vector", lambda e: e.reciprocal(r2_b[:, 0:nb],
                                                  z12_b[:, 0:nb]))
            P.drain("vector")
            return P.val("s_qs")

        def equant_ladder(F, out_ap, bufs, qs_v, v_waits=()):
            t_b, w_b, b_b, c2_b, shs_b = (bufs["t"], bufs["w"], bufs["b"],
                                          bufs["c2"], bufs["shs"])
            z12_b, r2_b = bufs["z12"], bufs["r2"]
            nb = F // 16
            P.wait("vector", "s_qs", qs_v)
            for sm, vv in v_waits:
                P.wait("vector", sm, vv)
            P.on("vector", lambda e: e.tensor_tensor(
                _x3(w_b, 0, F), _x3(t_b, 0, F), _bc(r2_b, 0, nb), op=AL.mult))
            P.on("vector", lambda e: e.tensor_scalar(
                b_b[:, 0:F], w_b[:, 0:F], 0.5, None, op0=AL.is_gt))
            for bnd in (1.5, 2.5, 3.5):
                P.on("vector", lambda e, bnd=bnd: e.scalar_tensor_tensor(
                    b_b[:, 0:F], w_b[:, 0:F], bnd, b_b[:, 0:F],
                    op0=AL.is_gt, op1=AL.add))
            P.on("vector", lambda e: e.tensor_scalar(
                c2_b[:, 0:F], w_b[:, 0:F], 10.0, 2.0,
                op0=AL.is_gt, op1=AL.mult))
            for bnd in (7.0, 5.0):
                P.on("vector", lambda e, bnd=bnd: e.scalar_tensor_tensor(
                    c2_b[:, 0:F], w_b[:, 0:F], bnd, c2_b[:, 0:F],
                    op0=AL.is_gt, op1=AL.add))
            P.on("vector", lambda e: e.scalar_tensor_tensor(
                w_b[:, 0:F], c2_b[:, 0:F], 2.0, b_b[:, 0:F],
                op0=AL.mult, op1=AL.add))
            P.on("vector", lambda e: e.tensor_tensor(
                _x3(t_b, 0, F), _x3(w_b, 0, F), _bc(z12_b, 0, nb),
                op=AL.mult))
            P.on("vector", lambda e: e.tensor_tensor(
                out_ap, t_b[:, 0:F], shs_b[:, 0:F], op=AL.mult))

        def emit_quant(F, x_ap, xi32_3d, out_ap, bufs, s_sv, sv_v, s_vs, s_g,
                       prev_g=0, pre_v_waits=(), pre_s_waits=()):
            t_b, w_b, b_b, c2_b, shs_b = (bufs["t"], bufs["w"], bufs["b"],
                                          bufs["c2"], bufs["shs"])
            bm_b, z12_b, r2_b = bufs["bm"], bufs["z12"], bufs["r2"]
            nb = F // 16
            Fv = (int(F * Q_SPLIT) // 16) * 16
            nv = Fv // 16

            def x3(ap, f0, f1):
                return ap[:, f0:f1].rearrange("p (b i) -> p b i", i=16)

            def bc(ap, b0, b1):
                return ap[:, b0:b1].unsqueeze(2).to_broadcast([128, b1 - b0, 16])

            if PORTABLE:
                qs_v = equant_stats(
                    F, x_ap, bufs,
                    s_waits=[(s_sv, sv_v)] + list(pre_s_waits),
                    v_waits=[(s_sv, sv_v)])
                equant_ladder(F, out_ap, bufs, qs_v,
                              v_waits=list(pre_v_waits))
                P.barrier_inc("vector", s_vs)
                return P.val(s_vs), P.val(s_g)

            P.wait("vector", s_sv, sv_v)
            for sm, vv in pre_s_waits:
                P.wait("vector", sm, vv)
            for sm, vv in pre_v_waits:
                P.wait("vector", sm, vv)
            P.on("vector", lambda e: e.tensor_scalar(
                t_b[:, 0:F].bitcast(I32), x_ap.bitcast(I32),
                absmask[:, 0:1], None, op0=AL.bitwise_and))
            P.on("vector", lambda e: e.tensor_reduce(
                bm_b[:, 0:nb].unsqueeze(2), x3(t_b, 0, F), axis=AX, op=AL.max))
            P.drain("vector")
            P.on("vector", lambda e: e.tensor_scalar(
                z12_b[:, 0:nb], bm_b[:, 0:nb], 6e-12, 1.0 / 12.0,
                op0=AL.max, op1=AL.mult))
            P.drain("vector")
            P.on("vector", lambda e: e.reciprocal(r2_b[:, 0:nb], z12_b[:, 0:nb]))
            P.drain("vector")
            P.on("vector", lambda e: e.scalar_tensor_tensor(
                x3(shs_b, 0, F).bitcast(I32), xi32_3d, sgnbit[:, 0:1],
                bc(z12_b, 0, nb).bitcast(I32),
                op0=AL.bitwise_and, op1=AL.bitwise_or))
            P.barrier_inc("vector", s_vs)
            g_start = P.val(s_vs)

            def half(eng, f0, f1, b0, b1):
                TS = lambda out, in_, s1, s2, o0, o1=AL.bypass: P.on(
                    eng, lambda e: e.tensor_scalar(out, in_, s1, s2, op0=o0, op1=o1))
                P.on(eng, lambda e: e.tensor_tensor(
                    x3(w_b, f0, f1), x3(t_b, f0, f1), bc(r2_b, b0, b1), op=AL.mult))
                P.drain(eng, hard=False)
                TS(t_b[:, f0:f1], w_b[:, f0:f1], C_RHI, C_RHI, AL.add, AL.subtract)
                TS(b_b[:, f0:f1], w_b[:, f0:f1], 0.5, C_RHI, AL.mult, AL.add)
                TS(c2_b[:, f0:f1], w_b[:, f0:f1], 10.0, 4.0, AL.is_gt, AL.mult)
                P.drain(eng, hard=False)
                TS(b_b[:, f0:f1], b_b[:, f0:f1], C_RHI + 2.0, 2.0,
                   AL.subtract, AL.mult)
                P.drain(eng, hard=False)
                TS(b_b[:, f0:f1], b_b[:, f0:f1], 0.0, 4.0, AL.max, AL.min)
                P.drain(eng, hard=False)
                if eng == "vector":
                    P.on(eng, lambda e: e.scalar_tensor_tensor(
                        t_b[:, f0:f1], t_b[:, f0:f1], 4.0, b_b[:, f0:f1],
                        op0=AL.min, op1=AL.add))
                else:
                    TS(t_b[:, f0:f1], t_b[:, f0:f1], 4.0, None, AL.min)
                    P.on(eng, lambda e: e.tensor_tensor(
                        t_b[:, f0:f1], t_b[:, f0:f1], b_b[:, f0:f1], op=AL.add))
                P.drain(eng, hard=False)
                P.on(eng, lambda e: e.tensor_tensor(
                    t_b[:, f0:f1], t_b[:, f0:f1], c2_b[:, f0:f1], op=AL.add))
                P.drain(eng, hard=False)
                P.on(eng, lambda e: e.tensor_tensor(
                    out_ap[:, f0:f1], t_b[:, f0:f1], shs_b[:, f0:f1], op=AL.mult))

            if Fv < F:
                P.wait("gpsimd", s_vs, g_start)
                half("gpsimd", Fv, F, nv, nb)
                P.barrier_inc("gpsimd", s_g)
            half("vector", 0, Fv, 0, nv)
            P.barrier_inc("vector", s_vs)
            return P.val(s_vs), P.val(s_g)

        with ExitStack() as pa:
            EA = pa.enter_context
            amask = EA(nc.sbuf_tensor("amask_s", [128, 3 * 384], BF16))
            maskc = EA(nc.sbuf_tensor("maskc_s", [128, SC_W], BF16))
            ident = EA(nc.sbuf_tensor("ident_s", [128, 128], F32))
            a4t = EA(nc.sbuf_tensor("a4t_s", [128, 128], F32))
            rowmask = EA(nc.sbuf_tensor("rowmask_s", [128, 1], F32))
            kinv = EA(nc.sbuf_tensor("kinv_s", [128, 512], F32))
            krow_s = EA(nc.sbuf_tensor("krow_s", [128, 512], F32))
            for dst, srcd in ((amask, amask_d), (maskc, maskc_d),
                              (ident, ident_d), (a4t, a4t_d),
                              (krow_s, krow_d),
                              (rowmask, rowmask_d)):
                P.dma("sync", dst[:], srcd, "s_cin")
            for eng in ("vector", "tensor", "scalar"):
                P.wait(eng, "s_cin")
            P.on("vector", lambda e: e.reciprocal(kinv[:], krow_s[:]))
            P.drain("vector")
            w_sb = {w: EA(nc.sbuf_tensor("w%sT_s" % w, [128, DC * D], BF16))
                    for w in ("v", "pv", "q", "k", "pk")}
            wstg = [EA(nc.sbuf_tensor("wstg%d" % r, [128, D], F32))
                    for r in range(2)]
            P.sem("s_wc")
            wc_hist = []

            def cast_chunk(dst_ap, src_dram):
                ci = len(wc_hist)
                if ci >= 2:
                    P.wait("sync", "s_wc", wc_hist[ci - 2])
                P.dma("sync", wstg[ci % 2][0:dst_ap.shape[0], :], src_dram, "s_w")
                P.wait("scalar", "s_w", P.val("s_w"))
                P.on("scalar", lambda e, d=dst_ap, s=wstg[ci % 2][0:dst_ap.shape[0], :]:
                     e.copy(d, s))
                P.barrier_inc("scalar", "s_wc")
                wc_hist.append(P.val("s_wc"))

            cast_chunk(vb_s[:], vb_d)
            cast_chunk(pvb_s[:], pvb_d)
            w_thresh = {}
            for w in ("v", "pv", "q", "k", "pk"):
                for c in range(DC):
                    cast_chunk(w_sb[w][:, c * D:(c + 1) * D],
                               wT_d[w][c * 128:(c + 1) * 128, :])
                w_thresh[w] = P.val("s_wc")

            x_ring = EA(nc.sbuf_tensor("x_ring", [128, XR * D], F32))
            xn_ring = EA(nc.sbuf_tensor("xn_ring", [128, 2 * D], F32))
            xnT_sw = EA(nc.sbuf_tensor("xnT_sw", [128, DC * 512], BF16))
            phT_sw = EA(nc.sbuf_tensor("phT_sw", [128, DC * 512], BF16))
            qT_sw = EA(nc.sbuf_tensor("qT_sw", [128, DC * 512], BF16))
            kT_r = EA(nc.sbuf_tensor("kT_r", [128, DC * R_K * 128], BF16))
            pkT_r = EA(nc.sbuf_tensor("pkT_r", [128, DC * R_PK * 128], BF16))
            v_r = [EA(nc.sbuf_tensor("v_r%d" % r, [128, D], BF16))
                   for r in range(R_V)]
            pv_r = [EA(nc.sbuf_tensor("pv_r%d" % r, [128, D], BF16))
                    for r in range(R_PV)]
            sc = EA(nc.sbuf_tensor("sc", [128, SC_W], F32))
            Ebuf = EA(nc.sbuf_tensor("Ebuf", [128, SC_W], BF16))
            Eu = EA(nc.sbuf_tensor("Eu", [128, 384], BF16))
            z1 = EA(nc.sbuf_tensor("z1", [128, 384], F32))
            z2 = EA(nc.sbuf_tensor("z2", [128, 384], F32))
            P0b = EA(nc.sbuf_tensor("P0b", [128, SC_W], F32))
            Pb = EA(nc.sbuf_tensor("Pb", [128, 2 * SC_W], F32))
            PT = EA(nc.sbuf_tensor("PT", [128, 2 * SC_W], BF16))
            m8 = EA(nc.sbuf_tensor("m8", [128, 8], F32))
            negM = EA(nc.sbuf_tensor("negM", [128, 1], F32))
            Zs = EA(nc.sbuf_tensor("Zs", [128, 1], F32))
            rz = EA(nc.sbuf_tensor("rz", [128, 2], F32))
            attn_stg = EA(nc.sbuf_tensor("attn_stg", [128, 2 * D], BF16))
            bnst = EA(nc.sbuf_tensor("bnst", [128, 12], F32))
            bnag = EA(nc.sbuf_tensor("bnag", [128, 2], F32))
            vep = EA(nc.sbuf_tensor("vep", [128, 1], F32))
            lnv = EA(nc.sbuf_tensor("lnv", [128, 1], F32))
            rstd = EA(nc.sbuf_tensor("rstd", [128, 1], F32))

            ps_proj = [EA(nc.psum_tensor("ps_proj%d" % i, [128, 512], F32))
                       for i in range(2)]
            ps_band = EA(nc.psum_tensor("ps_band", [128, 512], F32))
            ps_pk = EA(nc.psum_tensor("ps_pk", [128, 512], F32))
            ps_tr = [EA(nc.psum_tensor("ps_tr%d" % i, [128, 512], F32))
                     for i in range(2)]
            ps_out = [EA(nc.psum_tensor("ps_out%d" % i, [128, 512], F32))
                      for i in range(2)]

            tr_uses = [0]
            proj_uses = [0]
            next_j = [0]
            ln_after = {}

            def ln_tile(i):
                xs = x_ring[:, (i % XR) * D:(i % XR + 1) * D]
                xd = xn_ring[:, (i % 2) * D:(i % 2 + 1) * D]
                P.wait("vector", "s_x", 16 * (i + 1))
                if i >= 2:
                    P.wait("vector", "s_pe_tr", tr_after_tile[i - 2])
                if i >= 4:
                    P.wait("vector", "s_xnd", 16 * (i - 3))
                P.on("vector", lambda e: e.bn_stats(bnst[:, 0:6], xs[:, 0:512]))
                P.on("vector", lambda e: e.bn_stats(bnst[:, 6:12], xs[:, 512:1024]))
                P.drain("vector")
                P.on("vector", lambda e: e.bn_aggr(bnag[:], bnst[:]))
                P.drain("vector")
                P.on("vector", lambda e: e.tensor_scalar(
                    vep[:], bnag[:, 1:2], epscol[:, 0:1], None, op0=AL.add))
                P.barrier_inc("vector", "s_v2s")
                P.wait("scalar", "s_v2s")
                P.on("scalar", lambda e: e.activation(lnv[:], vep[:], AF.Ln))
                P.drain("scalar")
                P.on("scalar", lambda e: e.activation(rstd[:], lnv[:], AF.Exp,
                                                      scale=-0.5))
                P.barrier_inc("scalar", "s_s2v")
                P.wait("vector", "s_s2v")
                P.on("vector", lambda e: e.tensor_scalar(
                    xd, xs, bnag[:, 0:1], rstd[:, 0:1],
                    op0=AL.subtract, op1=AL.mult))
                P.barrier_inc("vector", "s_ln")
                ln_after[i] = P.val("s_ln")
                if i >= 2:
                    P.wait("sync", "s_ln", ln_after[i])
                    P.dma("sync", xn_hbm[(i - 2) * 128:(i - 1) * 128, :], xd,
                          "s_xnd")

            tr_after_tile = {}

            def tr_group(srcs, dst_ap, kind, pe_waits=()):
                bank = ps_tr[tr_uses[0] % 2]
                if tr_uses[0] >= 2:
                    P.wait("tensor", "s_evtr", tr_uses[0] - 1)
                for sm, vv in pe_waits:
                    P.wait("tensor", sm, vv)
                n = len(srcs)
                for q_i, src in enumerate(srcs):
                    d = bank[:, q_i * 128:(q_i + 1) * 128]
                    if kind == "transpose":
                        P.on("tensor", lambda e, d=d, s=src: e.transpose(d, s, ident[:]))
                    elif kind == "transpose_bf":
                        P.on("tensor", lambda e, d=d, s=src: e.transpose(d, s, identb[:]))
                    else:
                        P.on("tensor", lambda e, d=d, s=src: e.matmul(
                            d, s, a4t[:], start=True, stop=True))
                pe_v = P.pe_inc("s_pe_tr")
                ev_eng = "vector" if kind == "matmul" else "scalar"
                P.wait(ev_eng, "s_pe_tr", pe_v)
                P.wait(ev_eng, "s_pe_proj", P.val("s_pe_proj"))
                P.wait(ev_eng, "s_pe_sc", P.val("s_pe_sc"))
                P.wait(ev_eng, "s_pe_pv", P.val("s_pe_pv"))
                if kind == "matmul":
                    P.on("vector", lambda e, b=bank, d=dst_ap, n=n:
                         e.tensor_tensor(
                             d,
                             b[:, 0:n * 128].rearrange("p (c w) -> p c w",
                                                       w=128),
                             kinv[:, 0:n * 128].rearrange("p (c w) -> p c w",
                                                          w=128),
                             op=AL.mult))
                    P.barrier_inc("vector", "s_evtr")
                else:
                    P.on("scalar", lambda e, b=bank, d=dst_ap, n=n:
                         e.copy(d, b[:, 0:n * 128]))
                    P.barrier_inc("scalar", "s_evtr")
                tr_uses[0] += 1

            def proj_group(mms, evac_fns, wait_w=None, consumer_waits=()):
                bank = ps_proj[proj_uses[0] % 2]
                if proj_uses[0] >= 2:
                    P.wait("tensor", "s_evproj", proj_uses[0] - 1)
                if wait_w is not None:
                    P.wait("tensor", "s_wc", wait_w)
                P.wait("tensor", "s_evtr", P.val("s_evtr"))
                n = len(mms)
                for q_i, (lhsT, rhs, ncols) in enumerate(mms):
                    P.on("tensor", lambda e, b=bank, l=lhsT, r=rhs, ncx=ncols,
                         st=(q_i == 0), sp=(q_i == n - 1):
                         e.matmul(b[:, 0:ncx], l, r, start=st, stop=sp))
                pe_v = P.pe_inc("s_pe_proj")
                P.wait("scalar", "s_pe_proj", pe_v)
                for sm, vv in consumer_waits:
                    P.wait("scalar", sm, vv)
                for f in evac_fns:
                    f(bank)
                P.barrier_inc("scalar", "s_evproj")
                proj_uses[0] += 1

            att_state = {}
            pend_att = [None]

            def attention_front(j):
                i = j + 2
                sw = sweeps[i // 4]
                qsl = i - sw[0]
                P.wait("tensor", "s_evproj", P.val("s_evproj"))
                if j > 0:
                    P.wait("tensor", "s_evsc", j)
                for slot0, idx0, L in _runs([i - 2, i - 1, i], R_K):
                    for c in range(DC):
                        P.on("tensor", lambda e, c=c, s0=slot0, i0=idx0, L=L:
                             e.matmul(
                                 ps_band[:, i0 * 128:(i0 + L) * 128],
                                 qT_sw[:, c * 512 + qsl * 128:
                                       c * 512 + (qsl + 1) * 128],
                                 kT_r[:, (c * R_K + s0) * 128:
                                      (c * R_K + s0 + L) * 128],
                                 start=(c == 0), stop=(c == DC - 1)))
                for slot0, idx0, L in _runs([i - 1, i], R_PK):
                    for c in range(DC):
                        P.on("tensor", lambda e, c=c, s0=slot0, i0=idx0, L=L:
                             e.matmul(
                                 ps_pk[:, i0 * 128:(i0 + L) * 128],
                                 qT_sw[:, c * 512 + qsl * 128:
                                       c * 512 + (qsl + 1) * 128],
                                 pkT_r[:, (c * R_PK + s0) * 128:
                                       (c * R_PK + s0 + L) * 128],
                                 start=(c == 0), stop=(c == DC - 1)))
                sc_v = P.pe_inc("s_pe_sc")

                P.wait("vector", "s_pe_sc", sc_v)
                mvar = j if j < 2 else 2
                P.on("vector", lambda e, mv=mvar: e.scalar_tensor_tensor(
                    sc[:, 0:384], ps_band[:, 0:384], 1.0,
                    amask[:, mv * 384:(mv + 1) * 384], op0=AL.mult, op1=AL.add))
                P.on("vector", lambda e: e.tensor_copy(sc[:, 384:SC_W],
                                                       ps_pk[:, 0:256]))
                P.barrier_inc("vector", "s_evsc")
                P.on("vector", lambda e: e.reduce_max(negM[:], sc[:], axis=AX,
                                                      negate=True))
                P.on("vector", lambda e: e.max(out=m8[:], in_=sc[:, 0:384]))
                P.drain("vector")
                P.on("vector", lambda e: e.match_replace(
                    out=z1[:], in_to_replace=m8[:], in_values=sc[:, 0:384],
                    imm_value=NEG))
                P.drain("vector")
                P.on("vector", lambda e: e.max(out=m8[:], in_=z1[:]))
                P.drain("vector")
                P.on("vector", lambda e: e.match_replace(
                    out=z2[:], in_to_replace=m8[:], in_values=z1[:],
                    imm_value=NEG))
                P.barrier_inc("vector", "s_vq")

                P.wait("scalar", "s_vq")
                P.on("scalar", lambda e: e.activation(Ebuf[:], sc[:], AF.Exp,
                                                      bias=negM[:, 0:1]))
                P.on("scalar", lambda e: e.activation(Eu[:], z2[:], AF.Exp,
                                                      bias=negM[:, 0:1]))
                P.barrier_inc("scalar", "s_exp")

                P.wait("vector", "s_exp")
                P.wait("vector", "s_pe_tr", P.val("s_pe_tr"))
                pb = Pb[:, (j % 2) * SC_W:(j % 2 + 1) * SC_W]
                P.on("vector", lambda e: e.tensor_mul(P0b[:], Ebuf[:], maskc[:]))
                P.drain("vector", hard=False)
                P.on("vector", lambda e, pb=pb: e.tensor_sub(
                    pb[:, 0:384], P0b[:, 0:384], Eu[:]))
                P.on("vector", lambda e, pb=pb: e.tensor_copy(
                    pb[:, 384:SC_W], P0b[:, 384:SC_W]))
                P.drain("vector", hard=False)
                P.on("vector", lambda e, pb=pb: e.reduce_sum(Zs[:], pb[:], axis=AX))
                P.drain("vector")
                if j >= 2:
                    P.wait("vector", "s_attn", 2 * (j - 1))
                P.on("vector", lambda e, j=j: e.reciprocal(rz[:, j % 2:j % 2 + 1],
                                                           Zs[:]))
                if debug:
                    P.on("vector", lambda e, j=j: e.tensor_copy(
                        dbgS[:, j:j + 1], Zs[:]))
                    if j == 2:
                        P.on("vector", lambda e: e.tensor_copy(
                            dbgB[:, 0:SC_W], sc[:]))
                        P.on("vector", lambda e: e.tensor_copy(
                            dbgB[:, SC_W:SC_W + 1], negM[:]))
                P.barrier_inc("vector", "s_vP")
                att_state[j] = P.val("s_vP")
                ptb = PT[:, (j % 2) * SC_W:(j % 2 + 1) * SC_W]
                tr_group([pb[:, cc * 128:(cc + 1) * 128] for cc in range(4)],
                         ptb[:, 0:512], "transpose",
                         pe_waits=(("s_vP", att_state[j]),))
                tr_group([pb[:, 512:640]], ptb[:, 512:640], "transpose")
                att_state[j] = (att_state[j], P.val("s_evtr"))

            def attention_back(j):
                i = j + 2
                vP_v, evtr_v = att_state[j]
                ptb = PT[:, (j % 2) * SC_W:(j % 2 + 1) * SC_W]
                P.wait("tensor", "s_evtr", evtr_v)
                if j > 0:
                    P.wait("tensor", "s_attn", 2 * j)
                vt = [v_r[(i - 2) % R_V], v_r[(i - 1) % R_V], v_r[i % R_V],
                      pv_r[(i - 1) % R_PV], pv_r[i % R_PV]]
                for half in range(2):
                    for cc in range(5):
                        P.on("tensor", lambda e, h2=half, cc=cc, vt=vt,
                             ptb=ptb: e.matmul(
                            ps_out[h2][:],
                            ptb[:, cc * 128:(cc + 1) * 128],
                            vt[cc][:, h2 * 512:(h2 + 1) * 512],
                            start=(cc == 0), stop=(cc == 4)))
                pv_v = P.pe_inc("s_pe_pv")

                ev_e = "vector" if j == 0 else "scalar"
                P.wait(ev_e, "s_pe_pv", pv_v)
                if j >= 2:
                    P.wait(ev_e, "s_adma", 16 * (j - 1))
                if j > 0:
                    P.wait("scalar", "s_vP", vP_v)
                for half in range(2):
                    dsta = attn_stg[:, (j % 2) * D + half * 512:
                                    (j % 2) * D + (half + 1) * 512]
                    if j == 0:
                        P.on("vector", lambda e, d=dsta, h2=half, j=j: e.tensor_scalar(
                            d, ps_out[h2][:], rz[:, j % 2:j % 2 + 1],
                            rowmask[:, 0:1], op0=AL.mult, op1=AL.mult))
                    else:
                        P.on("scalar", lambda e, d=dsta, h2=half, j=j:
                             e.activation(d, ps_out[h2][:], AF.Identity,
                                          scale=rz[:, j % 2:j % 2 + 1]))
                P.barrier_inc(ev_e, "s_attn", 2)
                P.wait("sync", "s_attn", P.val("s_attn"))
                P.dma("sync", attn_hbm[j * 128:(j + 1) * 128, :],
                      attn_stg[:, (j % 2) * D:(j % 2 + 1) * D], "s_adma")

            ln_emitted = set()

            def do_ln(i):
                if i in ln_emitted:
                    return
                ln_emitted.add(i)
                if i >= XR:
                    P.wait("sync", "s_ln", LN0 + (i - XR + 1))
                P.dma("sync", x_ring[:, (i % XR) * D:(i % XR + 1) * D],
                      x_ext_d[i * 128:(i + 1) * 128, :], "s_x")
                ln_tile(i)

            for si_, sweep in enumerate(sweeps):
                sw0 = sweep[0]
                sw_w = len(sweep) * 128
                for i in sweep:
                    do_ln(i)
                    sslot = i - sw0
                    xn_t = xn_ring[:, (i % 2) * D:(i % 2 + 1) * D]
                    for r in range(4):
                        is_ph = r >= 2
                        c0 = 4 * (r - 2) if is_ph else 4 * r
                        srcs = [xn_t[:, (c0 + q) * 128:(c0 + q + 1) * 128]
                                for q in range(4)]
                        dst_t = phT_sw if is_ph else xnT_sw
                        dap = dst_t[:].rearrange("p (c w) -> p c w", w=512)[
                            :, c0:c0 + 4, sslot * 128:(sslot + 1) * 128]
                        tr_group(srcs, dap, "matmul" if is_ph else "transpose",
                                 pe_waits=(("s_ln", ln_after[i]),))
                    tr_after_tile[i] = P.val("s_pe_tr")
                    for w, ringlist, rn, bias_ap, srcT in (
                            ("v", v_r, R_V, vb_s, xnT_sw),
                            ("pv", pv_r, R_PV, pvb_s, phT_sw)):
                        for half in range(2):
                            mms = [(srcT[:, c * 512 + sslot * 128:
                                         c * 512 + (sslot + 1) * 128],
                                    w_sb[w][:, c * D + half * 512:
                                            c * D + (half + 1) * 512], 512)
                                   for c in range(DC)]
                            mms.append((ones1[0:1, :],
                                        bias_ap[0:1, half * 512:(half + 1) * 512],
                                        512))
                            dstv = ringlist[i % rn][:, half * 512:(half + 1) * 512]

                            def mk_ev(d):
                                return lambda bank: P.on(
                                    "scalar",
                                    lambda e, b=bank, d=d: e.copy(d, b[:]))
                            proj_group(mms, [mk_ev(dstv)], wait_w=w_thresh[w],
                                       consumer_waits=(("s_pe_pv",
                                                        P.val("s_pe_pv")),))
                for w, bias_sb, scl in (("q", bq_s, 1.0 / 32.0),
                                        ("k", bk_s, 1.0),
                                        ("pk", bpk_s, 1.0)):
                    srcT = phT_sw if w == "pk" else xnT_sw
                    ring = {"q": None, "k": R_K, "pk": R_PK}[w]
                    for m in range(DC):
                        mms = [(w_sb[w][:, c * D + m * 128:c * D + (m + 1) * 128],
                                srcT[:, c * 512:c * 512 + sw_w], sw_w)
                               for c in range(DC)]
                        evs = []
                        if w == "q":
                            dsts = [(qT_sw[:, m * 512:m * 512 + sw_w], 0, sw_w)]
                        else:
                            dstT = kT_r if w == "k" else pkT_r
                            dsts = []
                            for slot0, idx0, L in _runs(sweep, ring):
                                dsts.append((dstT[:, (m * ring + slot0) * 128:
                                                  (m * ring + slot0 + L) * 128],
                                             idx0 * 128, L * 128))

                        def mk_ev(d, o, ln_, b=bias_sb, m=m, scl=scl):
                            return lambda bank: P.on(
                                "scalar",
                                lambda e, bk_=bank, d=d, o=o, ln_=ln_:
                                e.activation(d, bk_[:, o:o + ln_], AF.Identity,
                                             bias=b[:, m:m + 1], scale=scl))
                        evs = [mk_ev(d, o, ln_) for d, o, ln_ in dsts]
                        proj_group(mms, evs, wait_w=w_thresh[w],
                                   consumer_waits=(("s_pe_sc",
                                                    P.val("s_pe_sc")),))
                if si_ + 1 < len(sweeps):
                    do_ln(sweeps[si_ + 1][0])
                while next_j[0] < NT_OWN and next_j[0] + 2 <= sweep[-1]:
                    attention_front(next_j[0])
                    if pend_att[0] is not None:
                        attention_back(pend_att[0])
                    pend_att[0] = next_j[0]
                    next_j[0] += 1

            if pend_att[0] is not None:
                attention_back(pend_att[0])
            P.phase_barrier("a", dma_waits=(("sync", "s_x"), ("sync", "s_adma"),
                                            ("sync", "s_xnd"), ("sync", "s_w")))

        wdqT = E(nc.sbuf_tensor("wdqT", [128, IC * D], BF16))
        wgqT = E(nc.sbuf_tensor("wgqT", [128, DC * I_FFN], BF16))
        wupqT = E(nc.sbuf_tensor("wupqT", [128, DC * I_FFN], BF16))

        with ExitStack() as pq:
            EQ = pq.enter_context
            QF = 2048
            stg = [EQ(nc.sbuf_tensor("qstg%d" % r, [128, QF], F32))
                   for r in range(2)]
            qb = {k: EQ(nc.sbuf_tensor("qb_" + k, [128, QF], F32))
                  for k in ("w", "b", "c2")}
            qb_t = [EQ(nc.sbuf_tensor("qb_t%d" % r, [128, QF], F32))
                    for r in range(2)]
            qb_shs = [EQ(nc.sbuf_tensor("qb_shs%d" % r, [128, QF],
                                     BF16 if PORTABLE else F32))
                      for r in range(2)]
            qsm = {k: EQ(nc.sbuf_tensor("qs_" + k, [128, QF // 16], F32))
                   for k in ("bm", "z12", "r2")}
            qout = [EQ(nc.sbuf_tensor("qout%d" % r, [128, QF], BF16))
                    for r in range(2)]
            ps_qtr = [EQ(nc.psum_tensor("ps_qtr%d" % i, [128, 1024], BF16))
                      for i in range(2)]
            qtr_uses = [0]
            qv_hist = []
            P.sem("s_qg")
            P.sem("s_qrd")

            def q_chunk(srcs, F, dst_fns):
                ci = len(qv_hist)
                slot = ci % 2
                if ci >= 2:
                    P.wait("sync", "s_qv", qv_hist[ci - 2][0])
                    P.wait("sync", "s_qg", qv_hist[ci - 2][1])
                off = 0
                for sd, wdt in srcs:
                    P.dma("sync", stg[slot][:, off:off + wdt], sd, "s_qd")
                    off += wdt
                dv = P.val("s_qd")
                pv_waits = []
                if ci >= 1:
                    pv_waits.append(("s_qpe", P.val("s_qpe")))
                prev_g = qv_hist[ci - 1][1] if ci >= 1 else 0
                ps_waits = []
                if ci >= 2:
                    ps_waits.append(("s_qv", qv_hist[ci - 2][0]))
                bufs = dict(qb, t=qb_t[ci % 2], shs=qb_shs[ci % 2], **qsm)
                vd, gd = emit_quant(
                    F, stg[slot][:, 0:F],
                    stg[slot][:, 0:F].rearrange("p (b i) -> p b i", i=16).bitcast(I32),
                    qout[0][:, 0:F], bufs, "s_qd", dv, "s_qv", "s_qg",
                    prev_g=prev_g, pre_v_waits=pv_waits, pre_s_waits=ps_waits)
                qv_hist.append((vd, gd))
                nP = F // 128
                gi = 0
                for g0 in range(0, nP, 8):
                    gl = min(8, nP - g0)
                    bank = ps_qtr[qtr_uses[0] % 2]
                    if qtr_uses[0] >= 2:
                        P.wait("tensor", "s_qev", qtr_uses[0] - 1)
                    P.wait("tensor", "s_qv", vd)
                    P.wait("tensor", "s_qg", gd)
                    for q_i in range(gl):
                        P.on("tensor", lambda e, b=bank, q_i=q_i, g0=g0:
                             e.transpose(b[:, q_i * 128:(q_i + 1) * 128],
                                         qout[0][:, (g0 + q_i) * 128:
                                                 (g0 + q_i + 1) * 128],
                                         identb[:]))
                    pe_v = P.pe_inc("s_qpe")
                    P.wait("scalar", "s_qpe", pe_v)
                    dst_fns[gi](bank, g0, gl)
                    P.barrier_inc("scalar", "s_qev")
                    qtr_uses[0] += 1
                    gi += 1

            dma_val = []
            qs_vals = []
            rd_vals = []
            vd_list = []
            qpe_at = {}
            pend_q = [None]

            def issue_dma(k, chunks):
                srcs, F, _ = chunks[k]
                slot = k % 2
                if k >= 2:
                    P.wait("sync", "s_qrd", rd_vals[k - 2])
                    P.wait("sync", "s_qs", qs_vals[k - 2])
                off = 0
                for sd, wdt in srcs:
                    P.dma("sync", stg[slot][:, off:off + wdt], sd, "s_qd")
                    off += wdt
                dma_val.append(P.val("s_qd"))

            def flush_q():
                if pend_q[0] is None:
                    return
                ci, slot, F, dst_fns = pend_q[0]
                pend_q[0] = None
                nP = F // 128
                gi = 0
                for g0 in range(0, nP, 8):
                    gl = min(8, nP - g0)
                    bank = ps_qtr[qtr_uses[0] % 2]
                    if qtr_uses[0] >= 2:
                        P.wait("tensor", "s_qev", qtr_uses[0] - 1)
                    P.wait("tensor", "s_qv", vd_list[ci])
                    for q_i in range(gl):
                        P.on("tensor", lambda e, b=bank, q_i=q_i, g0=g0,
                             slot=slot:
                             e.transpose(b[:, q_i * 128:(q_i + 1) * 128],
                                         qout[slot][:, (g0 + q_i) * 128:
                                                    (g0 + q_i + 1) * 128],
                                         identb[:]))
                    pe_v = P.pe_inc("s_qpe")
                    P.wait("scalar", "s_qpe", pe_v)
                    dst_fns[gi](bank, g0, gl)
                    P.barrier_inc("scalar", "s_qev")
                    qtr_uses[0] += 1
                    gi += 1
                qpe_at[ci] = P.val("s_qpe")

            def run_chunks(chunks):
                issue_dma(0, chunks)
                if len(chunks) > 1:
                    issue_dma(1, chunks)
                for ci, (srcs, F, dst_fns) in enumerate(chunks):
                    slot = ci % 2
                    bufs = dict(qb, t=qb_t[slot], shs=qb_shs[slot], **qsm)
                    s_w = [("s_qd", dma_val[ci])]
                    if ci >= 2:
                        s_w.append(("s_qv", vd_list[ci - 2]))
                    qs_v = equant_stats(F, stg[slot][:, 0:F], bufs,
                                        v_waits=[("s_qd", dma_val[ci])],
                                        s_waits=s_w, rd_sem="s_qrd")
                    qs_vals.append(qs_v)
                    rd_vals.append(P.val("s_qrd"))
                    if ci + 2 < len(chunks):
                        issue_dma(ci + 2, chunks)
                    flush_q()
                    lv = []
                    if ci >= 2:
                        lv.append(("s_qpe", qpe_at[ci - 2]))
                    equant_ladder(F, qout[slot][:, 0:F], bufs, qs_v,
                                  v_waits=lv)
                    P.barrier_inc("vector", "s_qv")
                    vd_list.append(P.val("s_qv"))
                    pend_q[0] = (ci, slot, F, dst_fns)
                flush_q()

            def mk_wd(hh, colh):
                def f(bank, g0, gl):
                    r0 = colh * 11 + g0
                    dap = wdqT[:].rearrange("p (r w) -> p r w", w=D)[
                        :, r0:r0 + gl, hh * 128:(hh + 1) * 128]
                    P.on("scalar", lambda e, b=bank, d=dap, gl=gl:
                         e.copy(d, b[:, 0:gl * 128]))
                return f

            def mk_w(dstT, r, base):
                def f(bank, g0, gl):
                    dap = dstT[:].rearrange("p (c w) -> p c w", w=I_FFN)[
                        :, g0 - base:g0 - base + gl, r * 128:(r + 1) * 128]
                    P.on("scalar", lambda e, b=bank, d=dap, gl=gl:
                         e.copy(d, b[:, 0:gl * 128]))
                return f

            all_chunks = []
            for hh in range(DC):
                for colh in range(2):
                    all_chunks.append((
                        [(wd_d[hh * 128:(hh + 1) * 128,
                               colh * 1408:(colh + 1) * 1408], 1408)],
                        1408, [mk_wd(hh, colh), mk_wd(hh, colh)]))
            for r in range(IC):
                all_chunks.append((
                    [(wg_d[r * 128:(r + 1) * 128, :], D),
                     (wu_d[r * 128:(r + 1) * 128, :], D)],
                    2048, [mk_w(wgqT, r, 0), mk_w(wupqT, r, 8)]))

            if PORTABLE:
                run_chunks(all_chunks)
            else:
                for srcs, F, dst_fns in all_chunks:
                    q_chunk(srcs, F, dst_fns)

            P.phase_barrier("q", dma_waits=(("sync", "s_qd"),))

        xqT = E(nc.sbuf_tensor("xqT", [128, NT_OWN * D], BF16))

        with ExitStack() as pb:
            EB = pb.enter_context
            xb_ring = EB(nc.sbuf_tensor("xb_ring", [128, 2 * D], F32))
            qb1 = {k: EB(nc.sbuf_tensor("b1_" + k, [128, D], F32))
                   for k in ("w", "b", "c2")}
            qb1_t = [EB(nc.sbuf_tensor("b1_t%d" % r, [128, D], F32))
                     for r in range(2)]
            qb1_shs = [EB(nc.sbuf_tensor("b1_shs%d" % r, [128, D],
                                      BF16 if PORTABLE else F32))
                       for r in range(2)]
            qs1 = {k: EB(nc.sbuf_tensor("b1s_" + k, [128, D // 16], F32))
                   for k in ("bm", "z12", "r2")}
            xq = [EB(nc.sbuf_tensor("xq%d" % r, [128, D], BF16))
                  for r in range(2)]
            ps_b1 = [EB(nc.psum_tensor("ps_b1%d" % i, [128, 1024], BF16))
                     for i in range(2)]
            b1tr = [0]
            bx0 = P.val("s_bx")
            vdone1 = []
            rd1 = []
            qs1v = []
            qpe1 = {}
            pend1 = [None]
            P.sem("s_bg")
            P.sem("s_brd")

            def b1_dma(t):
                if t >= NT_OWN:
                    return
                if t >= 2:
                    P.wait("sync", "s_brd", rd1[t - 2])
                    P.wait("sync", "s_qs", qs1v[t - 2])
                P.dma("sync", xb_ring[:, (t % 2) * D:(t % 2 + 1) * D],
                      xn_hbm[t * 128:(t + 1) * 128, :], "s_bx")

            def flush_b1():
                if pend1[0] is None:
                    return
                pt, slot = pend1[0]
                pend1[0] = None
                bank = ps_b1[b1tr[0] % 2]
                if b1tr[0] >= 2:
                    P.wait("tensor", "s_bev", b1tr[0] - 1)
                P.wait("tensor", "s_bv", vdone1[pt])
                for q_i in range(8):
                    P.on("tensor", lambda e, b=bank, q_i=q_i, slot=slot:
                         e.transpose(b[:, q_i * 128:(q_i + 1) * 128],
                                     xq[slot][:, q_i * 128:(q_i + 1) * 128],
                                     identb[:]))
                pe_v = P.pe_inc("s_bpe")
                P.wait("scalar", "s_bpe", pe_v)
                dap = xqT[:].rearrange("p (t2 c w) -> p t2 c w",
                                       t2=NT_OWN, w=128)[:, pt, 0:8, :]
                P.on("scalar", lambda e, b=bank, d=dap: e.copy(d, b[:]))
                P.barrier_inc("scalar", "s_bev")
                b1tr[0] += 1
                qpe1[pt] = P.val("s_bpe")

            b1_dma(0)
            b1_dma(1)
            for t in range(NT_OWN):
                xnb_t = xb_ring[:, (t % 2) * D:(t % 2 + 1) * D]
                bufs = dict(qb1, t=qb1_t[t % 2], shs=qb1_shs[t % 2], **qs1)
                s_w = [("s_bx", bx0 + 16 * (t + 1))]
                if t >= 2:
                    s_w.append(("s_bv", vdone1[t - 2]))
                qs_v = equant_stats(D, xnb_t, bufs,
                                    v_waits=[("s_bx", bx0 + 16 * (t + 1))],
                                    s_waits=s_w, rd_sem="s_brd")
                qs1v.append(qs_v)
                rd1.append(P.val("s_brd"))
                b1_dma(t + 2)
                flush_b1()
                lv = []
                if t >= 2:
                    lv.append(("s_bpe", qpe1[t - 2]))
                equant_ladder(D, xq[t % 2][:], bufs, qs_v, v_waits=lv)
                P.barrier_inc("vector", "s_bv")
                vdone1.append(P.val("s_bv"))
                pend1[0] = (t, t % 2)
            flush_b1()

            P.phase_barrier("b1", dma_waits=(("sync", "s_bx"),))

        with ExitStack() as p2:
            E2 = p2.enter_context
            x2_ring = E2(nc.sbuf_tensor("x2_ring", [128, 2 * D], F32))
            attn_stg2 = E2(nc.sbuf_tensor("attn_stg2", [128, 2 * D], BF16))
            oacc = E2(nc.sbuf_tensor("oacc", [128, D], F32))
            ostg = E2(nc.sbuf_tensor("ostg", [128, 2 * D], F32))
            sg = [E2(nc.sbuf_tensor("sg%d" % r, [128, 512], F32)) for r in range(2)]
            hbuf = [E2(nc.sbuf_tensor("hbuf%d" % r, [128, 512], F32))
                    for r in range(2)]
            hq = [E2(nc.sbuf_tensor("hq%d" % r, [128, 512], BF16))
                  for r in range(3)]
            hqT = [E2(nc.sbuf_tensor("hqT%d" % r, [128, 512], BF16))
                   for r in range(2)]
            q2b = {k: E2(nc.sbuf_tensor("b2_" + k, [128, 512], F32))
                   for k in ("w", "b", "c2")}
            q2t = [E2(nc.sbuf_tensor("b2_t%d" % r, [128, 512], F32))
                   for r in range(2)]
            q2shs = [E2(nc.sbuf_tensor("b2_shs%d" % r, [128, 512],
                                    BF16 if PORTABLE else F32))
                     for r in range(2)]
            q2s = {k: E2(nc.sbuf_tensor("b2s_" + k, [128, 32], F32))
                   for k in ("bm", "z12", "r2")}
            ps_gate = [E2(nc.psum_tensor("ps_gate%d" % i, [128, 512], F32))
                       for i in range(2)]
            ps_up = [E2(nc.psum_tensor("ps_up%d" % i, [128, 512], F32))
                     for i in range(2)]
            ps_dn = [E2(nc.psum_tensor("ps_dn%d" % i, [128, 512], F32))
                     for i in range(2)]
            ps_htr = [E2(nc.psum_tensor("ps_htr%d" % i, [128, 1024], BF16))
                      for i in range(2)]

            slices = [(n * 512, min(512, I_FFN - n * 512))
                      for n in range((I_FFN + 511) // 512)]
            NS = len(slices)
            bx1 = P.val("s_bx")
            oacc_done = []
            silu_inc = {}
            hmul_inc = {}
            vdone = {}
            absdone = {}
            tr_inc = {}
            evac_inc = {}
            down_inc = {}
            ostg_inc = {}

            pend = [None]

            def flush_pend():
                if pend[0] is None:
                    return
                (pt, psi, pgsi, pp, pnI, pi0) = pend[0]
                pend[0] = None
                P.wait("tensor", "s_bv", vdone[pgsi])
                if pgsi >= 2:
                    P.wait("tensor", "s_bev", evac_inc[pgsi - 2])
                hp = pgsi % 3
                for q_i in range(pnI):
                    P.on("tensor", lambda e, q_i=q_i, p=pp, hp=hp: e.transpose(
                        ps_htr[p][:, q_i * 128:(q_i + 1) * 128],
                        hq[hp][:, q_i * 128:(q_i + 1) * 128], identb[:]))
                tr_inc[pgsi] = P.pe_inc("s_bpe")
                P.wait("scalar", "s_bpe", tr_inc[pgsi])
                P.on("scalar", lambda e, nI=pnI, p=pp: e.copy(
                    hqT[p][:, 0:nI * 128], ps_htr[p][:, 0:nI * 128]))
                P.barrier_inc("scalar", "s_bev")
                evac_inc[pgsi] = P.val("s_bev")
                P.wait("tensor", "s_bev", evac_inc[pgsi])
                if psi == 0 and pt >= 1:
                    P.wait("tensor", "s_bv", ostg_inc[pt - 1])
                for half in range(2):
                    for cc in range(pnI):
                        r = pi0 // 128 + cc
                        P.on("tensor", lambda e, h2=half, cc=cc, r=r, p=pp,
                             st=(psi == 0 and cc == 0),
                             sp=(psi == NS - 1 and cc == pnI - 1):
                             e.matmul(
                                 ps_dn[h2][:],
                                 hqT[p][:, cc * 128:(cc + 1) * 128],
                                 wdqT[:, r * D + h2 * 512:
                                      r * D + (h2 + 1) * 512],
                                 start=st, stop=sp))
                down_inc[pgsi] = P.pe_inc("s_bpe")

            def b2_dma(t):
                if t >= NT_OWN:
                    return
                if t >= 2:
                    P.wait("sync", "s_bv", oacc_done[t - 2])
                P.dma("sync", x2_ring[:, (t % 2) * D:(t % 2 + 1) * D],
                      x_ext_d[(HALO + t * 128):(HALO + (t + 1) * 128), :], "s_bx")
                P.dma("sync", attn_stg2[:, (t % 2) * D:(t % 2 + 1) * D],
                      attn_hbm[t * 128:(t + 1) * 128, :], "s_bx")

            gu_inc = {}
            NGSI = NT_OWN * NS

            def emit_gu(g2):
                if g2 >= NGSI:
                    return
                t2, si2 = divmod(g2, NS)
                i0, iw = slices[si2]
                p2 = g2 % 2
                if g2 >= 2:
                    P.wait("tensor", "s_bs", silu_inc[g2 - 2])
                    P.wait("tensor", "s_bv", hmul_inc[g2 - 2])
                for q_i in range(DC):
                    P.on("tensor", lambda e, q_i=q_i, t=t2, i0=i0, iw=iw, p=p2:
                         e.matmul(
                             ps_gate[p][:, 0:iw],
                             xqT[:, (t * DC + q_i) * 128:
                                 (t * DC + q_i + 1) * 128],
                             wgqT[:, q_i * I_FFN + i0:q_i * I_FFN + i0 + iw],
                             start=(q_i == 0), stop=(q_i == DC - 1)))
                for q_i in range(DC):
                    P.on("tensor", lambda e, q_i=q_i, t=t2, i0=i0, iw=iw, p=p2:
                         e.matmul(
                             ps_up[p][:, 0:iw],
                             xqT[:, (t * DC + q_i) * 128:
                                 (t * DC + q_i + 1) * 128],
                             wupqT[:, q_i * I_FFN + i0:q_i * I_FFN + i0 + iw],
                             start=(q_i == 0), stop=(q_i == DC - 1)))
                gu_inc[g2] = P.pe_inc("s_bpe")

            b2_dma(0)
            b2_dma(1)
            emit_gu(0)
            for t in range(NT_OWN):
                P.wait("vector", "s_bx", bx1 + 32 * (t + 1))
                P.on("vector", lambda e, t=t: e.tensor_add(
                    oacc[:],
                    x2_ring[:, (t % 2) * D:(t % 2 + 1) * D],
                    attn_stg2[:, (t % 2) * D:(t % 2 + 1) * D]))
                P.barrier_inc("vector", "s_bv")
                oacc_done.append(P.val("s_bv"))
                b2_dma(t + 2)

                for si, (i0, iw) in enumerate(slices):
                    gsi = t * NS + si
                    p = gsi % 2
                    nI = iw // 128
                    gu_v = gu_inc[gsi]
                    P.wait("scalar", "s_bpe", gu_v)
                    if gsi >= 2:
                        P.wait("scalar", "s_bv", hmul_inc[gsi - 2])
                    P.on("scalar", lambda e, iw=iw, p=p: e.activation(
                        sg[p][:, 0:iw], ps_gate[p][:, 0:iw], AF.Silu))
                    P.barrier_inc("scalar", "s_bs")
                    silu_inc[gsi] = P.val("s_bs")
                    P.wait("vector", "s_bs", silu_inc[gsi])
                    if gsi >= 2:
                        P.wait("vector", "s_bv", vdone[gsi - 2])
                    P.on("vector", lambda e, iw=iw, p=p: e.tensor_mul(
                        hbuf[p][:, 0:iw], sg[p][:, 0:iw], ps_up[p][:, 0:iw]))
                    P.barrier_inc("vector", "s_bv")
                    hmul_inc[gsi] = P.val("s_bv")
                    pvw = []
                    if gsi >= 3:
                        pvw.append(("s_bpe", tr_inc[gsi - 3]))
                    psw = []
                    if gsi >= 2:
                        psw.append(("s_bv", vdone[gsi - 2]))
                    vd, gd = emit_quant(
                        iw, hbuf[p][:, 0:iw],
                        hbuf[p][:, 0:iw].rearrange("p (b i) -> p b i", i=16)
                        .bitcast(I32),
                        hq[gsi % 3][:, 0:iw],
                        dict(q2b, t=q2t[p], shs=q2shs[p], **q2s),
                        "s_bv", hmul_inc[gsi], "s_bv", "s_bg",
                        pre_v_waits=pvw, pre_s_waits=psw)
                    vdone[gsi] = vd
                    emit_gu(gsi + 1)
                    flush_pend()
                    pend[0] = (t, si, gsi, p, nI, i0)

                flush_pend()
                P.wait("vector", "s_bpe", down_inc[t * NS + NS - 1])
                if t >= 2:
                    P.wait("vector", "s_out", 16 * (t - 1))
                for half in range(2):
                    P.on("vector", lambda e, h2=half, t=t: e.scalar_tensor_tensor(
                        ostg[:, (t % 2) * D + h2 * 512:
                             (t % 2) * D + (h2 + 1) * 512],
                        ps_dn[h2][:], 1.0,
                        oacc[:, h2 * 512:(h2 + 1) * 512],
                        op0=AL.mult, op1=AL.add))
                P.barrier_inc("vector", "s_bv")
                ostg_inc[t] = P.val("s_bv")
                P.wait("sync", "s_bv", ostg_inc[t])
                P.dma("sync", out_d[t * 128:(t + 1) * 128, :],
                      ostg[:, (t % 2) * D:(t % 2 + 1) * D], "s_out")

            if debug:
                for t in range(NT_OWN):
                    P.dma("sync", dbg_attn[t * 128:(t + 1) * 128, :],
                          attn_hbm[t * 128:(t + 1) * 128, :], "s_out")
                P.dma("sync", dbg_small, dbgS[:], "s_out")
                P.dma("sync", dbg_big, dbgB[:], "s_out")
            P.wait("sync", "s_out", P.val("s_out"))
            P.wait("gpsimd", "s_out", P.val("s_out"))

        with nc.Block() as block:
            @block.sync
            def _(e):
                for fn in P.ops["sync"]:
                    fn(e)

            @block.gpsimd
            def _(e):
                for fn in P.ops["gpsimd"]:
                    fn(e)

            @block.vector
            def _(e):
                for fn in P.ops["vector"]:
                    fn(e)

            @block.scalar
            def _(e):
                for fn in P.ops["scalar"]:
                    fn(e)

            @block.tensor
            def _(e):
                for fn in P.ops["tensor"]:
                    fn(e)

    nc.finalize()
    return nc


def make_in_maps(inputs, n_own):
    x = np.ascontiguousarray(np.asarray(inputs["L0_attn_h"], dtype=np.float32))
    consts_a = make_host_constants(True)
    consts_b = make_host_constants(False)
    wTs = {"w" + w + "T": np.ascontiguousarray(
        np.asarray(inputs["L0_attn_%s_w" % w]).astype(np.float32).T)
        for w in ("q", "k", "v", "pk", "pv")}
    b128 = {}
    for w, nm in (("q", "bq"), ("k", "bk"), ("pk", "bpk")):
        b128[nm] = np.ascontiguousarray(
            np.asarray(inputs["L0_attn_%s_b" % w]).astype(np.float32)
            .reshape(DC, 128).T)
    vb = np.asarray(inputs["L0_attn_v_b"]).astype(np.float32).reshape(1, D)
    pvb = np.asarray(inputs["L0_attn_pv_b"]).astype(np.float32).reshape(1, D)
    wg = np.ascontiguousarray(np.asarray(inputs["L0_ffn_w_gate"], dtype=np.float32))
    wu = np.ascontiguousarray(np.asarray(inputs["L0_ffn_w_up"], dtype=np.float32))
    wd = np.ascontiguousarray(np.asarray(inputs["L0_ffn_w_down"], dtype=np.float32))

    in_maps = []
    n_half = x.shape[1] // 2
    assert n_half == n_own
    for core in range(8):
        b, h = core // 2, core % 2
        if h == 0:
            halo = np.zeros((HALO, D), np.float32)
            xo = x[b, 0:n_half]
        else:
            halo = x[b, n_half - HALO:n_half]
            xo = x[b, n_half:]
        x_ext = np.ascontiguousarray(np.concatenate([halo, xo], axis=0))
        cc = consts_a if h == 0 else consts_b
        m = dict(x_ext=x_ext, vb=vb, pvb=pvb, wg=wg, wu=wu, wd=wd,
                 amask=cc["amask"], maskc=cc["maskc"], ident=cc["ident"],
                 identb=cc["identb"], a4t=cc["a4t"], krow=cc["krow"],
                 ones1=cc["ones1"],
                 epscol=cc["epscol"], rowmask=cc["rowmask"],
                 sgnbit=cc["sgnbit"], absmask=cc["absmask"])
        m.update(wTs)
        m.update(b128)
        in_maps.append(m)
    return in_maps


_NC_CACHE = {}


def kernel(**inputs):
    n_own = S // 2
    if n_own not in _NC_CACHE:
        _NC_CACHE[n_own] = build_nc(n_own)
    nc = _NC_CACHE[n_own]
    in_maps = make_in_maps(inputs, n_own)
    res = run_bass_kernel_spmd(nc, in_maps, core_ids=list(range(8)))
    out = np.empty((B, S, D), np.float32)
    for core in range(8):
        b, h = core // 2, core % 2
        out[b, h * n_own:(h + 1) * n_own] = res.results[core]["out"]
    return out


# version 30
# speedup vs baseline: 1.0066x; 1.0066x over previous
from contextlib import ExitStack

import numpy as np
import ml_dtypes

import concourse.bass as bass
import concourse.mybir as mybir
from concourse.bass_utils import run_bass_kernel_spmd

F32 = mybir.dt.float32
BF16 = mybir.dt.bfloat16
I32 = mybir.dt.int32
AX = mybir.AxisListType.X
AL = mybir.AluOpType
AF = mybir.ActivationFunctionType

B, S, D = 4, 4096, 1024
DC = D // 128
I_FFN = 2816
IC = I_FFN // 128
T = 128
HALO = 256
CHUNK = 32
NEG = -1.0e30
C_RHI = float(np.float32(1.5 * 2 ** 23))
SIGNBIT = 0x80000000 - 2 ** 32

R_K = 6
R_PK = 5
R_V = 7
R_PV = 6
XR = 2

SC_W = 640

GUARDS = False
G_WRAP = True
G_DRAIN = True
G_DMAW = False
Q_SPLIT = 1.0
G_DRAIN_ELEM = False
PORTABLE = True

ENGINES = ("sync", "gpsimd", "vector", "scalar", "tensor")


def _runs(tiles, ring):
    out = []
    start = 0
    for k in range(1, len(tiles) + 1):
        if k == len(tiles) or tiles[k] % ring != (tiles[k - 1] % ring) + 1:
            out.append((tiles[start] % ring, start, k - start))
            start = k
    return out


class Prog:

    def __init__(self, nc, stack):
        self.nc = nc
        self.stack = stack
        self.ops = {e: [] for e in ENGINES}
        self.lastop = {e: None for e in ENGINES}
        self.sems = {}

    def sem(self, name):
        h = self.stack.enter_context(self.nc.semaphore(name))
        self.sems[name] = [h, 0]

    def inc(self, name, k=1):
        s = self.sems[name]
        s[1] += k
        return s[1]

    def val(self, name):
        return self.sems[name][1]

    def h(self, name):
        return self.sems[name][0]

    def on(self, eng, fn, is_op=True):
        self.ops[eng].append(fn)
        self.lastop[eng] = fn if is_op else None

    def wait(self, eng, name, v=None):
        if v is None:
            v = self.val(name)
        if v <= 0:
            return
        h = self.h(name)
        self.on(eng, lambda e, h=h, v=v: e.wait_ge(h, v), is_op=False)

    def dma(self, eng, dst, src, sem):
        h = self.h(sem)
        if GUARDS or G_DMAW:
            v = self.val(sem)
            if v > 0:
                self.on(eng, lambda e, h=h, v=v: e.wait_ge(h, v), is_op=False)
        self.inc(sem, 16)
        self.on(eng, lambda e, d=dst, s=src, h=h: e.dma_start(d, s).then_inc(h, 16),
                is_op=False)

    def barrier_inc(self, eng, name, k=1):
        h = self.h(name)
        self.inc(name, k)
        if not GUARDS and self.lastop[eng] is not None and                 self.ops[eng] and self.ops[eng][-1] is self.lastop[eng]:
            prev = self.ops[eng].pop()
            self.ops[eng].append(
                lambda e, p=prev, h=h, k=k: p(e).then_inc(h, k))
            self.lastop[eng] = None
        else:
            def f(e, h=h, k=k):
                e.drain()
                e.sem_inc(h, k)
            self.on(eng, f, is_op=False)

    def drain(self, eng, hard=True):
        if GUARDS or (G_DRAIN and eng != "gpsimd" and (hard or G_DRAIN_ELEM)):
            self.on(eng, lambda e: e.drain(), is_op=False)

    def pe_inc(self, name, k=1):
        self.barrier_inc("tensor", name, k)
        return self.val(name)

    def phase_barrier(self, tag, dma_waits=()):
        name = "s_ph_" + tag
        self.sem(name)
        for eng, semn in dma_waits:
            self.wait(eng, semn)
        for eng in ENGINES:
            self.barrier_inc(eng, name)
        v = self.val(name)
        for eng in ENGINES:
            self.wait(eng, name, v)


def make_host_constants(typeA):
    dt_b = ml_dtypes.bfloat16
    am = np.zeros((128, 3, 384), np.float32)
    dlt = np.arange(128)[:, None]
    j = np.arange(384)[None, :]
    band = (j >= dlt) & (j <= dlt + 255)
    for v in range(3):
        valid = band.copy()
        if typeA and v == 0:
            valid &= (j >= 256)
        if typeA and v == 1:
            valid &= (j >= 128)
        am[:, v, :] = np.where(valid, 0.0, NEG)
    amask = am.reshape(128, 3 * 384).astype(dt_b)

    mc = np.zeros((128, SC_W), np.float32)
    mc[:, 0:384][band] = 1.0
    for d_i in range(128):
        mc[d_i, d_i + 255] = 2.0
    mc[0, 384 + 127] = 1.0
    for d_i in range(1, 128):
        mc[d_i, 512 + d_i - 1] = 1.0
    maskc = mc.astype(dt_b)

    ident = np.eye(128, dtype=np.float32)
    identb = np.eye(128).astype(dt_b)
    a4t = np.zeros((128, 128), np.float32)
    for i in range(128):
        for jj in range(128):
            if i // CHUNK == jj // CHUNK and i % CHUNK <= jj % CHUNK:
                a4t[i, jj] = 1.0
    krow = np.tile((np.arange(512) % CHUNK + 1).astype(np.float32)[None, :],
                   (128, 1))
    ones1 = np.ones((1, 128), dt_b)
    sgnbit = np.full((128, 1), -2147483648, np.int32)
    absmask = np.full((128, 1), 0x7FFFFFFF, np.int32)
    epscol = np.full((128, 1), 1e-5, np.float32)
    rowmask = np.ones((128, 1), np.float32)
    if typeA:
        rowmask[0, 0] = 0.0
    return dict(amask=amask, maskc=maskc, ident=ident, identb=identb,
                a4t=a4t, krow=krow, ones1=ones1, epscol=epscol,
                rowmask=rowmask, sgnbit=sgnbit, absmask=absmask)


def build_nc(n_own, debug=False):
    n_ext = n_own + HALO
    NT_OWN = n_own // T
    NT_EXT = n_ext // T
    sweeps = []
    t0 = 0
    while t0 < NT_EXT:
        sweeps.append(list(range(t0, min(t0 + 4, NT_EXT))))
        t0 += 4

    nc = bass.Bass("TRN2", target_bir_lowering=False,
                   dynamic_dma_scratch_size=4096,
                   detect_race_conditions=GUARDS)
    global GUARDS_USED
    GUARDS_USED = GUARDS

    def din(name, shape, dtype=F32):
        return nc.dram_tensor(name, list(shape), dtype, kind="ExternalInput").ap()

    x_ext_d = din("x_ext", (n_ext, D))
    wT_d = {w: din("w" + w + "T", (D, D)) for w in ("q", "k", "v", "pk", "pv")}
    bq_d = din("bq", (128, DC))
    bk_d = din("bk", (128, DC))
    bpk_d = din("bpk", (128, DC))
    vb_d = din("vb", (1, D))
    pvb_d = din("pvb", (1, D))
    wg_d = din("wg", (I_FFN, D))
    wu_d = din("wu", (I_FFN, D))
    wd_d = din("wd", (D, I_FFN))
    amask_d = din("amask", (128, 3 * 384), BF16)
    maskc_d = din("maskc", (128, SC_W), BF16)
    ident_d = din("ident", (128, 128))
    identb_d = din("identb", (128, 128), BF16)
    a4t_d = din("a4t", (128, 128))
    krow_d = din("krow", (128, 512))
    ones1_d = din("ones1", (1, 128), BF16)
    sgnbit_d = din("sgnbit", (128, 1), I32)
    absmask_d = din("absmask", (128, 1), I32)
    epscol_d = din("epscol", (128, 1))
    rowmask_d = din("rowmask", (128, 1))
    out_d = nc.dram_tensor("out", [n_own, D], F32, kind="ExternalOutput").ap()
    attn_hbm = nc.dram_tensor("attn_hbm", [n_own, D], BF16, kind="Internal").ap()
    xn_hbm = nc.dram_tensor("xn_hbm", [n_own, D], F32, kind="Internal").ap()
    dbg_attn = (nc.dram_tensor("dbg_attn", [n_own, D], BF16,
                               kind="ExternalOutput").ap() if debug else None)
    dbg_small = (nc.dram_tensor("dbg_small", [128, 48], F32,
                                kind="ExternalOutput").ap() if debug else None)
    dbg_big = (nc.dram_tensor("dbg_big", [128, SC_W + 20], F32,
                              kind="ExternalOutput").ap() if debug else None)

    with ExitStack() as top:
        E = top.enter_context
        P = Prog(nc, top)

        identb = E(nc.sbuf_tensor("identb_s", [128, 128], BF16))
        ones1 = E(nc.sbuf_tensor("ones1_s", [1, 128], BF16))
        sgnbit = E(nc.sbuf_tensor("sgnbit_s", [128, 1], I32))
        absmask = E(nc.sbuf_tensor("absmask_s", [128, 1], I32))
        epscol = E(nc.sbuf_tensor("epscol_s", [128, 1], F32))
        bq_s = E(nc.sbuf_tensor("bq_s", [128, DC], F32))
        bk_s = E(nc.sbuf_tensor("bk_s", [128, DC], F32))
        bpk_s = E(nc.sbuf_tensor("bpk_s", [128, DC], F32))
        vb_s = E(nc.sbuf_tensor("vb_s", [1, D], BF16))
        pvb_s = E(nc.sbuf_tensor("pvb_s", [1, D], BF16))
        dbgS = E(nc.sbuf_tensor("dbgS", [128, 48], F32)) if debug else None
        dbgB = E(nc.sbuf_tensor("dbgB", [128, SC_W + 20], F32)) if debug else None

        for s in ("s_cin", "s_w", "s_x", "s_ln", "s_v2s", "s_s2v",
                  "s_pe_tr", "s_evtr", "s_pe_proj", "s_evproj",
                  "s_pe_sc", "s_evsc", "s_vq", "s_exp", "s_vP",
                  "s_pe_pv", "s_attn", "s_adma",
                  "s_qd", "s_qv", "s_qpe", "s_qev", "s_xnd",
                  "s_bx", "s_bv", "s_bs", "s_bpe", "s_bev", "s_out", "s_qs"):
            P.sem(s)

        for dst, src in ((identb, identb_d), (ones1, ones1_d),
                         (epscol, epscol_d),
                         (sgnbit, sgnbit_d), (absmask, absmask_d),
                         (bq_s, bq_d), (bk_s, bk_d), (bpk_s, bpk_d)):
            P.dma("sync", dst[:], src, "s_cin")

        P.wait("vector", "s_cin")
        P.on("vector", lambda e: e.tensor_scalar_mul(bq_s[:], bq_s[:], 1.0 / 32.0))
        P.barrier_inc("vector", "s_ln")
        LN0 = P.val("s_ln")

        def _x3(ap, f0, f1):
            return ap[:, f0:f1].rearrange("p (b i) -> p b i", i=16)

        def _bc(ap, b0, b1):
            return ap[:, b0:b1].unsqueeze(2).to_broadcast([128, b1 - b0, 16])

        def equant_stats(F, x_ap, bufs, v_waits=(), s_waits=(), rd_sem=None):
            t_b, shs_b = bufs["t"], bufs["shs"]
            bm_b, z12_b, r2_b = bufs["bm"], bufs["z12"], bufs["r2"]
            nb = F // 16
            for sm, vv in s_waits:
                P.wait("scalar", sm, vv)
            P.on("scalar", lambda e: e.activation(t_b[:, 0:F], x_ap, AF.Abs))
            P.on("scalar", lambda e: e.activation(shs_b[:, 0:F], x_ap, AF.Sign))
            P.barrier_inc("scalar", "s_qs")
            for sm, vv in v_waits:
                P.wait("vector", sm, vv)
            P.on("vector", lambda e: e.tensor_reduce(
                bm_b[:, 0:nb].unsqueeze(2), _x3(x_ap, 0, F), axis=AX,
                op=AL.max, apply_absolute_value=True))
            P.drain("vector")
            if rd_sem is not None:
                hh = P.h(rd_sem)
                P.inc(rd_sem, 1)
                P.on("vector", lambda e, hh=hh: e.sem_inc(hh, 1), is_op=False)
            P.on("vector", lambda e: e.tensor_scalar(
                z12_b[:, 0:nb], bm_b[:, 0:nb], 6e-12, 1.0 / 12.0,
                op0=AL.max, op1=AL.mult))
            P.drain("vector")
            P.on("vector", lambda e: e.reciprocal(r2_b[:, 0:nb],
                                                  z12_b[:, 0:nb]))
            P.drain("vector")
            return P.val("s_qs")

        def equant_ladder(F, out_ap, bufs, qs_v, v_waits=()):
            t_b, w_b, b_b, c2_b, shs_b = (bufs["t"], bufs["w"], bufs["b"],
                                          bufs["c2"], bufs["shs"])
            z12_b, r2_b = bufs["z12"], bufs["r2"]
            nb = F // 16
            P.wait("vector", "s_qs", qs_v)
            for sm, vv in v_waits:
                P.wait("vector", sm, vv)
            P.on("vector", lambda e: e.tensor_tensor(
                _x3(w_b, 0, F), _x3(t_b, 0, F), _bc(r2_b, 0, nb), op=AL.mult))
            P.on("vector", lambda e: e.tensor_scalar(
                b_b[:, 0:F], w_b[:, 0:F], 0.5, None, op0=AL.is_gt))
            for bnd in (1.5, 2.5, 3.5):
                P.on("vector", lambda e, bnd=bnd: e.scalar_tensor_tensor(
                    b_b[:, 0:F], w_b[:, 0:F], bnd, b_b[:, 0:F],
                    op0=AL.is_gt, op1=AL.add))
            P.on("vector", lambda e: e.tensor_scalar(
                c2_b[:, 0:F], w_b[:, 0:F], 10.0, 2.0,
                op0=AL.is_gt, op1=AL.mult))
            for bnd in (7.0, 5.0):
                P.on("vector", lambda e, bnd=bnd: e.scalar_tensor_tensor(
                    c2_b[:, 0:F], w_b[:, 0:F], bnd, c2_b[:, 0:F],
                    op0=AL.is_gt, op1=AL.add))
            P.on("vector", lambda e: e.scalar_tensor_tensor(
                w_b[:, 0:F], c2_b[:, 0:F], 2.0, b_b[:, 0:F],
                op0=AL.mult, op1=AL.add))
            P.on("vector", lambda e: e.tensor_tensor(
                _x3(t_b, 0, F), _x3(w_b, 0, F), _bc(z12_b, 0, nb),
                op=AL.mult))
            P.on("vector", lambda e: e.tensor_tensor(
                out_ap, t_b[:, 0:F], shs_b[:, 0:F], op=AL.mult))

        def emit_quant(F, x_ap, xi32_3d, out_ap, bufs, s_sv, sv_v, s_vs, s_g,
                       prev_g=0, pre_v_waits=(), pre_s_waits=()):
            t_b, w_b, b_b, c2_b, shs_b = (bufs["t"], bufs["w"], bufs["b"],
                                          bufs["c2"], bufs["shs"])
            bm_b, z12_b, r2_b = bufs["bm"], bufs["z12"], bufs["r2"]
            nb = F // 16
            Fv = (int(F * Q_SPLIT) // 16) * 16
            nv = Fv // 16

            def x3(ap, f0, f1):
                return ap[:, f0:f1].rearrange("p (b i) -> p b i", i=16)

            def bc(ap, b0, b1):
                return ap[:, b0:b1].unsqueeze(2).to_broadcast([128, b1 - b0, 16])

            if PORTABLE:
                qs_v = equant_stats(
                    F, x_ap, bufs,
                    s_waits=[(s_sv, sv_v)] + list(pre_s_waits),
                    v_waits=[(s_sv, sv_v)])
                equant_ladder(F, out_ap, bufs, qs_v,
                              v_waits=list(pre_v_waits))
                P.barrier_inc("vector", s_vs)
                return P.val(s_vs), P.val(s_g)

            P.wait("vector", s_sv, sv_v)
            for sm, vv in pre_s_waits:
                P.wait("vector", sm, vv)
            for sm, vv in pre_v_waits:
                P.wait("vector", sm, vv)
            P.on("vector", lambda e: e.tensor_scalar(
                t_b[:, 0:F].bitcast(I32), x_ap.bitcast(I32),
                absmask[:, 0:1], None, op0=AL.bitwise_and))
            P.on("vector", lambda e: e.tensor_reduce(
                bm_b[:, 0:nb].unsqueeze(2), x3(t_b, 0, F), axis=AX, op=AL.max))
            P.drain("vector")
            P.on("vector", lambda e: e.tensor_scalar(
                z12_b[:, 0:nb], bm_b[:, 0:nb], 6e-12, 1.0 / 12.0,
                op0=AL.max, op1=AL.mult))
            P.drain("vector")
            P.on("vector", lambda e: e.reciprocal(r2_b[:, 0:nb], z12_b[:, 0:nb]))
            P.drain("vector")
            P.on("vector", lambda e: e.scalar_tensor_tensor(
                x3(shs_b, 0, F).bitcast(I32), xi32_3d, sgnbit[:, 0:1],
                bc(z12_b, 0, nb).bitcast(I32),
                op0=AL.bitwise_and, op1=AL.bitwise_or))
            P.barrier_inc("vector", s_vs)
            g_start = P.val(s_vs)

            def half(eng, f0, f1, b0, b1):
                TS = lambda out, in_, s1, s2, o0, o1=AL.bypass: P.on(
                    eng, lambda e: e.tensor_scalar(out, in_, s1, s2, op0=o0, op1=o1))
                P.on(eng, lambda e: e.tensor_tensor(
                    x3(w_b, f0, f1), x3(t_b, f0, f1), bc(r2_b, b0, b1), op=AL.mult))
                P.drain(eng, hard=False)
                TS(t_b[:, f0:f1], w_b[:, f0:f1], C_RHI, C_RHI, AL.add, AL.subtract)
                TS(b_b[:, f0:f1], w_b[:, f0:f1], 0.5, C_RHI, AL.mult, AL.add)
                TS(c2_b[:, f0:f1], w_b[:, f0:f1], 10.0, 4.0, AL.is_gt, AL.mult)
                P.drain(eng, hard=False)
                TS(b_b[:, f0:f1], b_b[:, f0:f1], C_RHI + 2.0, 2.0,
                   AL.subtract, AL.mult)
                P.drain(eng, hard=False)
                TS(b_b[:, f0:f1], b_b[:, f0:f1], 0.0, 4.0, AL.max, AL.min)
                P.drain(eng, hard=False)
                if eng == "vector":
                    P.on(eng, lambda e: e.scalar_tensor_tensor(
                        t_b[:, f0:f1], t_b[:, f0:f1], 4.0, b_b[:, f0:f1],
                        op0=AL.min, op1=AL.add))
                else:
                    TS(t_b[:, f0:f1], t_b[:, f0:f1], 4.0, None, AL.min)
                    P.on(eng, lambda e: e.tensor_tensor(
                        t_b[:, f0:f1], t_b[:, f0:f1], b_b[:, f0:f1], op=AL.add))
                P.drain(eng, hard=False)
                P.on(eng, lambda e: e.tensor_tensor(
                    t_b[:, f0:f1], t_b[:, f0:f1], c2_b[:, f0:f1], op=AL.add))
                P.drain(eng, hard=False)
                P.on(eng, lambda e: e.tensor_tensor(
                    out_ap[:, f0:f1], t_b[:, f0:f1], shs_b[:, f0:f1], op=AL.mult))

            if Fv < F:
                P.wait("gpsimd", s_vs, g_start)
                half("gpsimd", Fv, F, nv, nb)
                P.barrier_inc("gpsimd", s_g)
            half("vector", 0, Fv, 0, nv)
            P.barrier_inc("vector", s_vs)
            return P.val(s_vs), P.val(s_g)

        with ExitStack() as pa:
            EA = pa.enter_context
            amask = EA(nc.sbuf_tensor("amask_s", [128, 3 * 384], BF16))
            maskc = EA(nc.sbuf_tensor("maskc_s", [128, SC_W], BF16))
            ident = EA(nc.sbuf_tensor("ident_s", [128, 128], F32))
            a4t = EA(nc.sbuf_tensor("a4t_s", [128, 128], F32))
            rowmask = EA(nc.sbuf_tensor("rowmask_s", [128, 1], F32))
            kinv = EA(nc.sbuf_tensor("kinv_s", [128, 512], F32))
            krow_s = EA(nc.sbuf_tensor("krow_s", [128, 512], F32))
            for dst, srcd in ((amask, amask_d), (maskc, maskc_d),
                              (ident, ident_d), (a4t, a4t_d),
                              (krow_s, krow_d),
                              (rowmask, rowmask_d)):
                P.dma("sync", dst[:], srcd, "s_cin")
            for eng in ("vector", "tensor", "scalar"):
                P.wait(eng, "s_cin")
            P.on("vector", lambda e: e.reciprocal(kinv[:], krow_s[:]))
            P.drain("vector")
            w_sb = {w: EA(nc.sbuf_tensor("w%sT_s" % w, [128, DC * D], BF16))
                    for w in ("v", "pv", "q", "k", "pk")}
            wstg = [EA(nc.sbuf_tensor("wstg%d" % r, [128, D], F32))
                    for r in range(2)]
            P.sem("s_wc")
            wc_hist = []

            def cast_chunk(dst_ap, src_dram):
                ci = len(wc_hist)
                if ci >= 2:
                    P.wait("sync", "s_wc", wc_hist[ci - 2])
                P.dma("sync", wstg[ci % 2][0:dst_ap.shape[0], :], src_dram, "s_w")
                P.wait("scalar", "s_w", P.val("s_w"))
                P.on("scalar", lambda e, d=dst_ap, s=wstg[ci % 2][0:dst_ap.shape[0], :]:
                     e.copy(d, s))
                P.barrier_inc("scalar", "s_wc")
                wc_hist.append(P.val("s_wc"))

            cast_chunk(vb_s[:], vb_d)
            cast_chunk(pvb_s[:], pvb_d)
            w_thresh = {}
            for w in ("v", "pv", "q", "k", "pk"):
                for c in range(DC):
                    cast_chunk(w_sb[w][:, c * D:(c + 1) * D],
                               wT_d[w][c * 128:(c + 1) * 128, :])
                w_thresh[w] = P.val("s_wc")

            x_ring = EA(nc.sbuf_tensor("x_ring", [128, XR * D], F32))
            xn_ring = EA(nc.sbuf_tensor("xn_ring", [128, 2 * D], F32))
            xnT_sw = EA(nc.sbuf_tensor("xnT_sw", [128, DC * 512], BF16))
            phT_sw = EA(nc.sbuf_tensor("phT_sw", [128, DC * 512], BF16))
            qT_sw = EA(nc.sbuf_tensor("qT_sw", [128, DC * 512], BF16))
            kT_r = EA(nc.sbuf_tensor("kT_r", [128, DC * R_K * 128], BF16))
            pkT_r = EA(nc.sbuf_tensor("pkT_r", [128, DC * R_PK * 128], BF16))
            v_r = [EA(nc.sbuf_tensor("v_r%d" % r, [128, D], BF16))
                   for r in range(R_V)]
            pv_r = [EA(nc.sbuf_tensor("pv_r%d" % r, [128, D], BF16))
                    for r in range(R_PV)]
            sc = EA(nc.sbuf_tensor("sc", [128, SC_W], F32))
            Ebuf = EA(nc.sbuf_tensor("Ebuf", [128, SC_W], BF16))
            Eu = EA(nc.sbuf_tensor("Eu", [128, 384], BF16))
            z1 = EA(nc.sbuf_tensor("z1", [128, 384], F32))
            z2 = EA(nc.sbuf_tensor("z2", [128, 384], F32))
            P0b = EA(nc.sbuf_tensor("P0b", [128, SC_W], F32))
            Pb = EA(nc.sbuf_tensor("Pb", [128, 2 * SC_W], F32))
            PT = EA(nc.sbuf_tensor("PT", [128, 2 * SC_W], BF16))
            m8 = EA(nc.sbuf_tensor("m8", [128, 8], F32))
            negM = EA(nc.sbuf_tensor("negM", [128, 1], F32))
            Zs = EA(nc.sbuf_tensor("Zs", [128, 1], F32))
            rz = EA(nc.sbuf_tensor("rz", [128, 2], F32))
            attn_stg = EA(nc.sbuf_tensor("attn_stg", [128, 2 * D], BF16))
            bnst = EA(nc.sbuf_tensor("bnst", [128, 12], F32))
            bnag = EA(nc.sbuf_tensor("bnag", [128, 2], F32))
            vep = EA(nc.sbuf_tensor("vep", [128, 1], F32))
            lnv = EA(nc.sbuf_tensor("lnv", [128, 1], F32))
            rstd = EA(nc.sbuf_tensor("rstd", [128, 1], F32))

            ps_proj = [EA(nc.psum_tensor("ps_proj%d" % i, [128, 512], F32))
                       for i in range(2)]
            ps_band = EA(nc.psum_tensor("ps_band", [128, 512], F32))
            ps_pk = EA(nc.psum_tensor("ps_pk", [128, 512], F32))
            ps_tr = [EA(nc.psum_tensor("ps_tr%d" % i, [128, 512], F32))
                     for i in range(2)]
            ps_out = [EA(nc.psum_tensor("ps_out%d" % i, [128, 512], F32))
                      for i in range(2)]

            tr_uses = [0]
            proj_uses = [0]
            next_j = [0]
            ln_after = {}

            def ln_tile(i):
                xs = x_ring[:, (i % XR) * D:(i % XR + 1) * D]
                xd = xn_ring[:, (i % 2) * D:(i % 2 + 1) * D]
                P.wait("vector", "s_x", 16 * (i + 1))
                if i >= 2:
                    P.wait("vector", "s_pe_tr", tr_after_tile[i - 2])
                if i >= 4:
                    P.wait("vector", "s_xnd", 16 * (i - 3))
                P.on("vector", lambda e: e.bn_stats(bnst[:, 0:6], xs[:, 0:512]))
                P.on("vector", lambda e: e.bn_stats(bnst[:, 6:12], xs[:, 512:1024]))
                P.drain("vector")
                P.on("vector", lambda e: e.bn_aggr(bnag[:], bnst[:]))
                P.drain("vector")
                P.on("vector", lambda e: e.tensor_scalar(
                    vep[:], bnag[:, 1:2], epscol[:, 0:1], None, op0=AL.add))
                P.barrier_inc("vector", "s_v2s")
                P.wait("scalar", "s_v2s")
                P.on("scalar", lambda e: e.activation(lnv[:], vep[:], AF.Ln))
                P.drain("scalar")
                P.on("scalar", lambda e: e.activation(rstd[:], lnv[:], AF.Exp,
                                                      scale=-0.5))
                P.barrier_inc("scalar", "s_s2v")
                P.wait("vector", "s_s2v")
                P.on("vector", lambda e: e.tensor_scalar(
                    xd, xs, bnag[:, 0:1], rstd[:, 0:1],
                    op0=AL.subtract, op1=AL.mult))
                P.barrier_inc("vector", "s_ln")
                ln_after[i] = P.val("s_ln")
                if i >= 2:
                    P.wait("sync", "s_ln", ln_after[i])
                    P.dma("sync", xn_hbm[(i - 2) * 128:(i - 1) * 128, :], xd,
                          "s_xnd")

            tr_after_tile = {}

            def tr_group(srcs, dst_ap, kind, pe_waits=()):
                bank = ps_tr[tr_uses[0] % 2]
                if tr_uses[0] >= 2:
                    P.wait("tensor", "s_evtr", tr_uses[0] - 1)
                for sm, vv in pe_waits:
                    P.wait("tensor", sm, vv)
                n = len(srcs)
                for q_i, src in enumerate(srcs):
                    d = bank[:, q_i * 128:(q_i + 1) * 128]
                    if kind == "transpose":
                        P.on("tensor", lambda e, d=d, s=src: e.transpose(d, s, ident[:]))
                    elif kind == "transpose_bf":
                        P.on("tensor", lambda e, d=d, s=src: e.transpose(d, s, identb[:]))
                    else:
                        P.on("tensor", lambda e, d=d, s=src: e.matmul(
                            d, s, a4t[:], start=True, stop=True))
                pe_v = P.pe_inc("s_pe_tr")
                ev_eng = "vector" if kind == "matmul" else "scalar"
                P.wait(ev_eng, "s_pe_tr", pe_v)
                P.wait(ev_eng, "s_pe_proj", P.val("s_pe_proj"))
                P.wait(ev_eng, "s_pe_sc", P.val("s_pe_sc"))
                P.wait(ev_eng, "s_pe_pv", P.val("s_pe_pv"))
                if kind == "matmul":
                    P.on("vector", lambda e, b=bank, d=dst_ap, n=n:
                         e.tensor_tensor(
                             d,
                             b[:, 0:n * 128].rearrange("p (c w) -> p c w",
                                                       w=128),
                             kinv[:, 0:n * 128].rearrange("p (c w) -> p c w",
                                                          w=128),
                             op=AL.mult))
                    P.barrier_inc("vector", "s_evtr")
                else:
                    P.on("scalar", lambda e, b=bank, d=dst_ap, n=n:
                         e.copy(d, b[:, 0:n * 128]))
                    P.barrier_inc("scalar", "s_evtr")
                tr_uses[0] += 1

            def proj_group(mms, evac_fns, wait_w=None, consumer_waits=()):
                bank = ps_proj[proj_uses[0] % 2]
                if proj_uses[0] >= 2:
                    P.wait("tensor", "s_evproj", proj_uses[0] - 1)
                if wait_w is not None:
                    P.wait("tensor", "s_wc", wait_w)
                P.wait("tensor", "s_evtr", P.val("s_evtr"))
                n = len(mms)
                for q_i, (lhsT, rhs, ncols) in enumerate(mms):
                    P.on("tensor", lambda e, b=bank, l=lhsT, r=rhs, ncx=ncols,
                         st=(q_i == 0), sp=(q_i == n - 1):
                         e.matmul(b[:, 0:ncx], l, r, start=st, stop=sp))
                pe_v = P.pe_inc("s_pe_proj")
                P.wait("scalar", "s_pe_proj", pe_v)
                for sm, vv in consumer_waits:
                    P.wait("scalar", sm, vv)
                for f in evac_fns:
                    f(bank)
                P.barrier_inc("scalar", "s_evproj")
                proj_uses[0] += 1

            att_state = {}
            pend_att = [None]

            def attention_front(j):
                i = j + 2
                sw = sweeps[i // 4]
                qsl = i - sw[0]
                P.wait("tensor", "s_evproj", P.val("s_evproj"))
                if j > 0:
                    P.wait("tensor", "s_evsc", j)
                for slot0, idx0, L in _runs([i - 2, i - 1, i], R_K):
                    for c in range(DC):
                        P.on("tensor", lambda e, c=c, s0=slot0, i0=idx0, L=L:
                             e.matmul(
                                 ps_band[:, i0 * 128:(i0 + L) * 128],
                                 qT_sw[:, c * 512 + qsl * 128:
                                       c * 512 + (qsl + 1) * 128],
                                 kT_r[:, (c * R_K + s0) * 128:
                                      (c * R_K + s0 + L) * 128],
                                 start=(c == 0), stop=(c == DC - 1)))
                for slot0, idx0, L in _runs([i - 1, i], R_PK):
                    for c in range(DC):
                        P.on("tensor", lambda e, c=c, s0=slot0, i0=idx0, L=L:
                             e.matmul(
                                 ps_pk[:, i0 * 128:(i0 + L) * 128],
                                 qT_sw[:, c * 512 + qsl * 128:
                                       c * 512 + (qsl + 1) * 128],
                                 pkT_r[:, (c * R_PK + s0) * 128:
                                       (c * R_PK + s0 + L) * 128],
                                 start=(c == 0), stop=(c == DC - 1)))
                sc_v = P.pe_inc("s_pe_sc")

                P.wait("vector", "s_pe_sc", sc_v)
                mvar = j if j < 2 else 2
                P.on("vector", lambda e, mv=mvar: e.scalar_tensor_tensor(
                    sc[:, 0:384], ps_band[:, 0:384], 1.0,
                    amask[:, mv * 384:(mv + 1) * 384], op0=AL.mult, op1=AL.add))
                P.on("vector", lambda e: e.tensor_copy(sc[:, 384:SC_W],
                                                       ps_pk[:, 0:256]))
                P.barrier_inc("vector", "s_evsc")
                P.on("vector", lambda e: e.reduce_max(negM[:], sc[:], axis=AX,
                                                      negate=True))
                P.on("vector", lambda e: e.max(out=m8[:], in_=sc[:, 0:384]))
                P.drain("vector")
                P.on("vector", lambda e: e.match_replace(
                    out=z1[:], in_to_replace=m8[:], in_values=sc[:, 0:384],
                    imm_value=NEG))
                P.drain("vector")
                P.on("vector", lambda e: e.max(out=m8[:], in_=z1[:]))
                P.drain("vector")
                P.on("vector", lambda e: e.match_replace(
                    out=z2[:], in_to_replace=m8[:], in_values=z1[:],
                    imm_value=NEG))
                P.barrier_inc("vector", "s_vq")

                P.wait("scalar", "s_vq")
                P.on("scalar", lambda e: e.activation(Ebuf[:], sc[:], AF.Exp,
                                                      bias=negM[:, 0:1]))
                P.on("scalar", lambda e: e.activation(Eu[:], z2[:], AF.Exp,
                                                      bias=negM[:, 0:1]))
                P.barrier_inc("scalar", "s_exp")

                P.wait("vector", "s_exp")
                P.wait("vector", "s_pe_tr", P.val("s_pe_tr"))
                pb = Pb[:, (j % 2) * SC_W:(j % 2 + 1) * SC_W]
                P.on("vector", lambda e: e.tensor_mul(P0b[:], Ebuf[:], maskc[:]))
                P.drain("vector", hard=False)
                P.on("vector", lambda e, pb=pb: e.tensor_sub(
                    pb[:, 0:384], P0b[:, 0:384], Eu[:]))
                P.on("vector", lambda e, pb=pb: e.tensor_copy(
                    pb[:, 384:SC_W], P0b[:, 384:SC_W]))
                P.drain("vector", hard=False)
                P.on("vector", lambda e, pb=pb: e.reduce_sum(Zs[:], pb[:], axis=AX))
                P.drain("vector")
                if j >= 2:
                    P.wait("vector", "s_attn", 2 * (j - 1))
                P.on("vector", lambda e, j=j: e.reciprocal(rz[:, j % 2:j % 2 + 1],
                                                           Zs[:]))
                if debug:
                    P.on("vector", lambda e, j=j: e.tensor_copy(
                        dbgS[:, j:j + 1], Zs[:]))
                    if j == 2:
                        P.on("vector", lambda e: e.tensor_copy(
                            dbgB[:, 0:SC_W], sc[:]))
                        P.on("vector", lambda e: e.tensor_copy(
                            dbgB[:, SC_W:SC_W + 1], negM[:]))
                P.barrier_inc("vector", "s_vP")
                att_state[j] = P.val("s_vP")
                ptb = PT[:, (j % 2) * SC_W:(j % 2 + 1) * SC_W]
                tr_group([pb[:, cc * 128:(cc + 1) * 128] for cc in range(4)],
                         ptb[:, 0:512], "transpose",
                         pe_waits=(("s_vP", att_state[j]),))
                tr_group([pb[:, 512:640]], ptb[:, 512:640], "transpose")
                att_state[j] = (att_state[j], P.val("s_evtr"))

            def attention_back(j):
                i = j + 2
                vP_v, evtr_v = att_state[j]
                ptb = PT[:, (j % 2) * SC_W:(j % 2 + 1) * SC_W]
                P.wait("tensor", "s_evtr", evtr_v)
                if j > 0:
                    P.wait("tensor", "s_attn", 2 * j)
                vt = [v_r[(i - 2) % R_V], v_r[(i - 1) % R_V], v_r[i % R_V],
                      pv_r[(i - 1) % R_PV], pv_r[i % R_PV]]
                for half in range(2):
                    for cc in range(5):
                        P.on("tensor", lambda e, h2=half, cc=cc, vt=vt,
                             ptb=ptb: e.matmul(
                            ps_out[h2][:],
                            ptb[:, cc * 128:(cc + 1) * 128],
                            vt[cc][:, h2 * 512:(h2 + 1) * 512],
                            start=(cc == 0), stop=(cc == 4)))
                pv_v = P.pe_inc("s_pe_pv")

                ev_e = "vector" if j == 0 else "scalar"
                P.wait(ev_e, "s_pe_pv", pv_v)
                if j >= 2:
                    P.wait(ev_e, "s_adma", 16 * (j - 1))
                if j > 0:
                    P.wait("scalar", "s_vP", vP_v)
                for half in range(2):
                    dsta = attn_stg[:, (j % 2) * D + half * 512:
                                    (j % 2) * D + (half + 1) * 512]
                    if j == 0:
                        P.on("vector", lambda e, d=dsta, h2=half, j=j: e.tensor_scalar(
                            d, ps_out[h2][:], rz[:, j % 2:j % 2 + 1],
                            rowmask[:, 0:1], op0=AL.mult, op1=AL.mult))
                    else:
                        P.on("scalar", lambda e, d=dsta, h2=half, j=j:
                             e.activation(d, ps_out[h2][:], AF.Identity,
                                          scale=rz[:, j % 2:j % 2 + 1]))
                P.barrier_inc(ev_e, "s_attn", 2)
                P.wait("sync", "s_attn", P.val("s_attn"))
                P.dma("sync", attn_hbm[j * 128:(j + 1) * 128, :],
                      attn_stg[:, (j % 2) * D:(j % 2 + 1) * D], "s_adma")

            ln_emitted = set()

            def do_ln(i):
                if i in ln_emitted:
                    return
                ln_emitted.add(i)
                if i >= XR:
                    P.wait("sync", "s_ln", LN0 + (i - XR + 1))
                P.dma("sync", x_ring[:, (i % XR) * D:(i % XR + 1) * D],
                      x_ext_d[i * 128:(i + 1) * 128, :], "s_x")
                ln_tile(i)

            for si_, sweep in enumerate(sweeps):
                sw0 = sweep[0]
                sw_w = len(sweep) * 128
                for i in sweep:
                    do_ln(i)
                    sslot = i - sw0
                    xn_t = xn_ring[:, (i % 2) * D:(i % 2 + 1) * D]
                    for r in range(4):
                        is_ph = r >= 2
                        c0 = 4 * (r - 2) if is_ph else 4 * r
                        srcs = [xn_t[:, (c0 + q) * 128:(c0 + q + 1) * 128]
                                for q in range(4)]
                        dst_t = phT_sw if is_ph else xnT_sw
                        dap = dst_t[:].rearrange("p (c w) -> p c w", w=512)[
                            :, c0:c0 + 4, sslot * 128:(sslot + 1) * 128]
                        tr_group(srcs, dap, "matmul" if is_ph else "transpose",
                                 pe_waits=(("s_ln", ln_after[i]),))
                    tr_after_tile[i] = P.val("s_pe_tr")
                    for w, ringlist, rn, bias_ap, srcT in (
                            ("v", v_r, R_V, vb_s, xnT_sw),
                            ("pv", pv_r, R_PV, pvb_s, phT_sw)):
                        for half in range(2):
                            mms = [(srcT[:, c * 512 + sslot * 128:
                                         c * 512 + (sslot + 1) * 128],
                                    w_sb[w][:, c * D + half * 512:
                                            c * D + (half + 1) * 512], 512)
                                   for c in range(DC)]
                            mms.append((ones1[0:1, :],
                                        bias_ap[0:1, half * 512:(half + 1) * 512],
                                        512))
                            dstv = ringlist[i % rn][:, half * 512:(half + 1) * 512]

                            def mk_ev(d):
                                return lambda bank: P.on(
                                    "scalar",
                                    lambda e, b=bank, d=d: e.copy(d, b[:]))
                            proj_group(mms, [mk_ev(dstv)], wait_w=w_thresh[w],
                                       consumer_waits=(("s_pe_pv",
                                                        P.val("s_pe_pv")),))
                for w, bias_sb, scl in (("q", bq_s, 1.0 / 32.0),
                                        ("k", bk_s, 1.0),
                                        ("pk", bpk_s, 1.0)):
                    srcT = phT_sw if w == "pk" else xnT_sw
                    ring = {"q": None, "k": R_K, "pk": R_PK}[w]
                    for m in range(DC):
                        mms = [(w_sb[w][:, c * D + m * 128:c * D + (m + 1) * 128],
                                srcT[:, c * 512:c * 512 + sw_w], sw_w)
                               for c in range(DC)]
                        evs = []
                        if w == "q":
                            dsts = [(qT_sw[:, m * 512:m * 512 + sw_w], 0, sw_w)]
                        else:
                            dstT = kT_r if w == "k" else pkT_r
                            dsts = []
                            for slot0, idx0, L in _runs(sweep, ring):
                                dsts.append((dstT[:, (m * ring + slot0) * 128:
                                                  (m * ring + slot0 + L) * 128],
                                             idx0 * 128, L * 128))

                        def mk_ev(d, o, ln_, b=bias_sb, m=m, scl=scl):
                            return lambda bank: P.on(
                                "scalar",
                                lambda e, bk_=bank, d=d, o=o, ln_=ln_:
                                e.activation(d, bk_[:, o:o + ln_], AF.Identity,
                                             bias=b[:, m:m + 1], scale=scl))
                        evs = [mk_ev(d, o, ln_) for d, o, ln_ in dsts]
                        proj_group(mms, evs, wait_w=w_thresh[w],
                                   consumer_waits=(("s_pe_sc",
                                                    P.val("s_pe_sc")),))
                if si_ + 1 < len(sweeps):
                    do_ln(sweeps[si_ + 1][0])
                while next_j[0] < NT_OWN and next_j[0] + 2 <= sweep[-1]:
                    attention_front(next_j[0])
                    if pend_att[0] is not None:
                        attention_back(pend_att[0])
                    pend_att[0] = next_j[0]
                    next_j[0] += 1

            if pend_att[0] is not None:
                attention_back(pend_att[0])
            P.phase_barrier("a", dma_waits=(("sync", "s_x"), ("sync", "s_adma"),
                                            ("sync", "s_xnd"), ("sync", "s_w")))

        wdqT = E(nc.sbuf_tensor("wdqT", [128, IC * D], BF16))
        wgqT = E(nc.sbuf_tensor("wgqT", [128, DC * I_FFN], BF16))
        wupqT = E(nc.sbuf_tensor("wupqT", [128, DC * I_FFN], BF16))

        with ExitStack() as pq:
            EQ = pq.enter_context
            QF = 2048
            stg = [EQ(nc.sbuf_tensor("qstg%d" % r, [128, QF], F32))
                   for r in range(2)]
            qb = {k: EQ(nc.sbuf_tensor("qb_" + k, [128, QF], F32))
                  for k in ("w", "b", "c2")}
            qb_t = [EQ(nc.sbuf_tensor("qb_t%d" % r, [128, QF], F32))
                    for r in range(2)]
            qb_shs = [EQ(nc.sbuf_tensor("qb_shs%d" % r, [128, QF],
                                     BF16 if PORTABLE else F32))
                      for r in range(2)]
            qsm = {k: EQ(nc.sbuf_tensor("qs_" + k, [128, QF // 16], F32))
                   for k in ("bm", "z12", "r2")}
            qout = [EQ(nc.sbuf_tensor("qout%d" % r, [128, QF], BF16))
                    for r in range(2)]
            ps_qtr = [EQ(nc.psum_tensor("ps_qtr%d" % i, [128, 1024], BF16))
                      for i in range(2)]
            qtr_uses = [0]
            qv_hist = []
            P.sem("s_qg")
            P.sem("s_qrd")

            def q_chunk(srcs, F, dst_fns):
                ci = len(qv_hist)
                slot = ci % 2
                if ci >= 2:
                    P.wait("sync", "s_qv", qv_hist[ci - 2][0])
                    P.wait("sync", "s_qg", qv_hist[ci - 2][1])
                off = 0
                for sd, wdt in srcs:
                    P.dma("sync", stg[slot][:, off:off + wdt], sd, "s_qd")
                    off += wdt
                dv = P.val("s_qd")
                pv_waits = []
                if ci >= 1:
                    pv_waits.append(("s_qpe", P.val("s_qpe")))
                prev_g = qv_hist[ci - 1][1] if ci >= 1 else 0
                ps_waits = []
                if ci >= 2:
                    ps_waits.append(("s_qv", qv_hist[ci - 2][0]))
                bufs = dict(qb, t=qb_t[ci % 2], shs=qb_shs[ci % 2], **qsm)
                vd, gd = emit_quant(
                    F, stg[slot][:, 0:F],
                    stg[slot][:, 0:F].rearrange("p (b i) -> p b i", i=16).bitcast(I32),
                    qout[0][:, 0:F], bufs, "s_qd", dv, "s_qv", "s_qg",
                    prev_g=prev_g, pre_v_waits=pv_waits, pre_s_waits=ps_waits)
                qv_hist.append((vd, gd))
                nP = F // 128
                gi = 0
                for g0 in range(0, nP, 8):
                    gl = min(8, nP - g0)
                    bank = ps_qtr[qtr_uses[0] % 2]
                    if qtr_uses[0] >= 2:
                        P.wait("tensor", "s_qev", qtr_uses[0] - 1)
                    P.wait("tensor", "s_qv", vd)
                    P.wait("tensor", "s_qg", gd)
                    for q_i in range(gl):
                        P.on("tensor", lambda e, b=bank, q_i=q_i, g0=g0:
                             e.transpose(b[:, q_i * 128:(q_i + 1) * 128],
                                         qout[0][:, (g0 + q_i) * 128:
                                                 (g0 + q_i + 1) * 128],
                                         identb[:]))
                    pe_v = P.pe_inc("s_qpe")
                    P.wait("scalar", "s_qpe", pe_v)
                    dst_fns[gi](bank, g0, gl)
                    P.barrier_inc("scalar", "s_qev")
                    qtr_uses[0] += 1
                    gi += 1

            dma_val = []
            qs_vals = []
            rd_vals = []
            vd_list = []
            qpe_at = {}
            pend_q = [None]

            def issue_dma(k, chunks):
                srcs, F, _ = chunks[k]
                slot = k % 2
                if k >= 2:
                    P.wait("sync", "s_qrd", rd_vals[k - 2])
                    P.wait("sync", "s_qs", qs_vals[k - 2])
                off = 0
                for sd, wdt in srcs:
                    P.dma("sync", stg[slot][:, off:off + wdt], sd, "s_qd")
                    off += wdt
                dma_val.append(P.val("s_qd"))

            def flush_q():
                if pend_q[0] is None:
                    return
                ci, slot, F, dst_fns = pend_q[0]
                pend_q[0] = None
                nP = F // 128
                gi = 0
                for g0 in range(0, nP, 8):
                    gl = min(8, nP - g0)
                    bank = ps_qtr[qtr_uses[0] % 2]
                    if qtr_uses[0] >= 2:
                        P.wait("tensor", "s_qev", qtr_uses[0] - 1)
                    P.wait("tensor", "s_qv", vd_list[ci])
                    for q_i in range(gl):
                        P.on("tensor", lambda e, b=bank, q_i=q_i, g0=g0,
                             slot=slot:
                             e.transpose(b[:, q_i * 128:(q_i + 1) * 128],
                                         qout[slot][:, (g0 + q_i) * 128:
                                                    (g0 + q_i + 1) * 128],
                                         identb[:]))
                    pe_v = P.pe_inc("s_qpe")
                    P.wait("scalar", "s_qpe", pe_v)
                    dst_fns[gi](bank, g0, gl)
                    P.barrier_inc("scalar", "s_qev")
                    qtr_uses[0] += 1
                    gi += 1
                qpe_at[ci] = P.val("s_qpe")

            def run_chunks(chunks):
                issue_dma(0, chunks)
                if len(chunks) > 1:
                    issue_dma(1, chunks)
                for ci, (srcs, F, dst_fns) in enumerate(chunks):
                    slot = ci % 2
                    bufs = dict(qb, t=qb_t[slot], shs=qb_shs[slot], **qsm)
                    s_w = [("s_qd", dma_val[ci])]
                    if ci >= 2:
                        s_w.append(("s_qv", vd_list[ci - 2]))
                    qs_v = equant_stats(F, stg[slot][:, 0:F], bufs,
                                        v_waits=[("s_qd", dma_val[ci])],
                                        s_waits=s_w, rd_sem="s_qrd")
                    qs_vals.append(qs_v)
                    rd_vals.append(P.val("s_qrd"))
                    if ci + 2 < len(chunks):
                        issue_dma(ci + 2, chunks)
                    flush_q()
                    lv = []
                    if ci >= 2:
                        lv.append(("s_qpe", qpe_at[ci - 2]))
                    equant_ladder(F, qout[slot][:, 0:F], bufs, qs_v,
                                  v_waits=lv)
                    P.barrier_inc("vector", "s_qv")
                    vd_list.append(P.val("s_qv"))
                    pend_q[0] = (ci, slot, F, dst_fns)
                flush_q()

            qsmB = {k: EQ(nc.sbuf_tensor("qs2_" + k, [128, QF // 16], F32))
                    for k in ("bm", "z12", "r2")}
            pend2 = []

            def flush_q2():
                while pend2:
                    pend_q[0] = pend2.pop(0)
                    flush_q()

            def run_chunks_paired(chunks):
                n = len(chunks)
                assert n % 2 == 0
                issue_dma(0, chunks)
                issue_dma(1, chunks)
                for ca in range(0, n, 2):
                    cb = ca + 1
                    (sa, Fa, dfa), (sb, Fb, dfb) = chunks[ca], chunks[cb]
                    bufsA = dict(qb, t=qb_t[0], shs=qb_shs[0], **qsm)
                    bufsB = dict(qb, t=qb_t[1], shs=qb_shs[1], **qsmB)
                    for ci, bufs, F in ((ca, bufsA, Fa), (cb, bufsB, Fb)):
                        P.wait("scalar", "s_qd", dma_val[ci])
                        if ci >= 2:
                            P.wait("scalar", "s_qv", vd_list[ci - 2])
                        t_b, shs_b = bufs["t"], bufs["shs"]
                        xap = stg[ci % 2][:, 0:F]
                        P.on("scalar", lambda e, t=t_b, x=xap, F=F:
                             e.activation(t[:, 0:F], x, AF.Abs))
                        P.on("scalar", lambda e, s2=shs_b, x=xap, F=F:
                             e.activation(s2[:, 0:F], x, AF.Sign))
                        P.barrier_inc("scalar", "s_qs")
                        qs_vals.append(P.val("s_qs"))
                    P.wait("vector", "s_qd", dma_val[cb])
                    for bufs, F, ci in ((bufsA, Fa, ca), (bufsB, Fb, cb)):
                        nb = F // 16
                        P.on("vector", lambda e, b=bufs, F=F, nb=nb, ci=ci:
                             e.tensor_reduce(
                                 b["bm"][:, 0:nb].unsqueeze(2),
                                 _x3(stg[ci % 2][:, 0:F], 0, F), axis=AX,
                                 op=AL.max, apply_absolute_value=True))
                    P.drain("vector")
                    hh2 = P.h("s_qrd")
                    P.inc("s_qrd", 1)
                    P.on("vector", lambda e, hh2=hh2: e.sem_inc(hh2, 1),
                         is_op=False)
                    rd_vals.append(P.val("s_qrd"))
                    rd_vals.append(P.val("s_qrd"))
                    for bufs, F in ((bufsA, Fa), (bufsB, Fb)):
                        nb = F // 16
                        P.on("vector", lambda e, b=bufs, nb=nb: e.tensor_scalar(
                            b["z12"][:, 0:nb], b["bm"][:, 0:nb], 6e-12,
                            1.0 / 12.0, op0=AL.max, op1=AL.mult))
                    P.drain("vector")
                    for bufs, F in ((bufsA, Fa), (bufsB, Fb)):
                        nb = F // 16
                        P.on("vector", lambda e, b=bufs, nb=nb: e.reciprocal(
                            b["r2"][:, 0:nb], b["z12"][:, 0:nb]))
                    P.drain("vector")
                    if ca + 2 < n:
                        issue_dma(ca + 2, chunks)
                    if cb + 2 < n:
                        issue_dma(cb + 2, chunks)
                    flush_q2()
                    for ci, bufs, F, dfs in ((ca, bufsA, Fa, dfa),
                                             (cb, bufsB, Fb, dfb)):
                        lv = []
                        if ci >= 2:
                            lv.append(("s_qpe", qpe_at[ci - 2]))
                        equant_ladder(F, qout[ci % 2][:, 0:F], bufs,
                                      qs_vals[ci], v_waits=lv)
                        P.barrier_inc("vector", "s_qv")
                        vd_list.append(P.val("s_qv"))
                        pend2.append((ci, ci % 2, F, dfs))
                flush_q2()

            def mk_wd(hh, colh):
                def f(bank, g0, gl):
                    r0 = colh * 11 + g0
                    dap = wdqT[:].rearrange("p (r w) -> p r w", w=D)[
                        :, r0:r0 + gl, hh * 128:(hh + 1) * 128]
                    P.on("scalar", lambda e, b=bank, d=dap, gl=gl:
                         e.copy(d, b[:, 0:gl * 128]))
                return f

            def mk_w(dstT, r, base):
                def f(bank, g0, gl):
                    dap = dstT[:].rearrange("p (c w) -> p c w", w=I_FFN)[
                        :, g0 - base:g0 - base + gl, r * 128:(r + 1) * 128]
                    P.on("scalar", lambda e, b=bank, d=dap, gl=gl:
                         e.copy(d, b[:, 0:gl * 128]))
                return f

            all_chunks = []
            for hh in range(DC):
                for colh in range(2):
                    all_chunks.append((
                        [(wd_d[hh * 128:(hh + 1) * 128,
                               colh * 1408:(colh + 1) * 1408], 1408)],
                        1408, [mk_wd(hh, colh), mk_wd(hh, colh)]))
            for r in range(IC):
                all_chunks.append((
                    [(wg_d[r * 128:(r + 1) * 128, :], D),
                     (wu_d[r * 128:(r + 1) * 128, :], D)],
                    2048, [mk_w(wgqT, r, 0), mk_w(wupqT, r, 8)]))

            if PORTABLE:
                run_chunks_paired(all_chunks)
            else:
                for srcs, F, dst_fns in all_chunks:
                    q_chunk(srcs, F, dst_fns)

            P.phase_barrier("q", dma_waits=(("sync", "s_qd"),))

        xqT = E(nc.sbuf_tensor("xqT", [128, NT_OWN * D], BF16))

        with ExitStack() as pb:
            EB = pb.enter_context
            xb_ring = EB(nc.sbuf_tensor("xb_ring", [128, 2 * D], F32))
            qb1 = {k: EB(nc.sbuf_tensor("b1_" + k, [128, D], F32))
                   for k in ("w", "b", "c2")}
            qb1_t = [EB(nc.sbuf_tensor("b1_t%d" % r, [128, D], F32))
                     for r in range(2)]
            qb1_shs = [EB(nc.sbuf_tensor("b1_shs%d" % r, [128, D],
                                      BF16 if PORTABLE else F32))
                       for r in range(2)]
            qs1 = {k: EB(nc.sbuf_tensor("b1s_" + k, [128, D // 16], F32))
                   for k in ("bm", "z12", "r2")}
            xq = [EB(nc.sbuf_tensor("xq%d" % r, [128, D], BF16))
                  for r in range(2)]
            ps_b1 = [EB(nc.psum_tensor("ps_b1%d" % i, [128, 1024], BF16))
                     for i in range(2)]
            b1tr = [0]
            bx0 = P.val("s_bx")
            vdone1 = []
            rd1 = []
            qs1v = []
            qpe1 = {}
            pend1 = [None]
            P.sem("s_bg")
            P.sem("s_brd")

            def b1_dma(t):
                if t >= NT_OWN:
                    return
                if t >= 2:
                    P.wait("sync", "s_brd", rd1[t - 2])
                    P.wait("sync", "s_qs", qs1v[t - 2])
                P.dma("sync", xb_ring[:, (t % 2) * D:(t % 2 + 1) * D],
                      xn_hbm[t * 128:(t + 1) * 128, :], "s_bx")

            def flush_b1():
                if pend1[0] is None:
                    return
                pt, slot = pend1[0]
                pend1[0] = None
                bank = ps_b1[b1tr[0] % 2]
                if b1tr[0] >= 2:
                    P.wait("tensor", "s_bev", b1tr[0] - 1)
                P.wait("tensor", "s_bv", vdone1[pt])
                for q_i in range(8):
                    P.on("tensor", lambda e, b=bank, q_i=q_i, slot=slot:
                         e.transpose(b[:, q_i * 128:(q_i + 1) * 128],
                                     xq[slot][:, q_i * 128:(q_i + 1) * 128],
                                     identb[:]))
                pe_v = P.pe_inc("s_bpe")
                P.wait("scalar", "s_bpe", pe_v)
                dap = xqT[:].rearrange("p (t2 c w) -> p t2 c w",
                                       t2=NT_OWN, w=128)[:, pt, 0:8, :]
                P.on("scalar", lambda e, b=bank, d=dap: e.copy(d, b[:]))
                P.barrier_inc("scalar", "s_bev")
                b1tr[0] += 1
                qpe1[pt] = P.val("s_bpe")

            b1_dma(0)
            b1_dma(1)
            for t in range(NT_OWN):
                xnb_t = xb_ring[:, (t % 2) * D:(t % 2 + 1) * D]
                bufs = dict(qb1, t=qb1_t[t % 2], shs=qb1_shs[t % 2], **qs1)
                s_w = [("s_bx", bx0 + 16 * (t + 1))]
                if t >= 2:
                    s_w.append(("s_bv", vdone1[t - 2]))
                qs_v = equant_stats(D, xnb_t, bufs,
                                    v_waits=[("s_bx", bx0 + 16 * (t + 1))],
                                    s_waits=s_w, rd_sem="s_brd")
                qs1v.append(qs_v)
                rd1.append(P.val("s_brd"))
                b1_dma(t + 2)
                flush_b1()
                lv = []
                if t >= 2:
                    lv.append(("s_bpe", qpe1[t - 2]))
                equant_ladder(D, xq[t % 2][:], bufs, qs_v, v_waits=lv)
                P.barrier_inc("vector", "s_bv")
                vdone1.append(P.val("s_bv"))
                pend1[0] = (t, t % 2)
            flush_b1()

            P.phase_barrier("b1", dma_waits=(("sync", "s_bx"),))

        with ExitStack() as p2:
            E2 = p2.enter_context
            x2_ring = E2(nc.sbuf_tensor("x2_ring", [128, 2 * D], F32))
            attn_stg2 = E2(nc.sbuf_tensor("attn_stg2", [128, 2 * D], BF16))
            oacc = E2(nc.sbuf_tensor("oacc", [128, D], F32))
            ostg = E2(nc.sbuf_tensor("ostg", [128, 2 * D], F32))
            sg = [E2(nc.sbuf_tensor("sg%d" % r, [128, 512], F32)) for r in range(2)]
            hbuf = [E2(nc.sbuf_tensor("hbuf%d" % r, [128, 512], F32))
                    for r in range(2)]
            hq = [E2(nc.sbuf_tensor("hq%d" % r, [128, 512], BF16))
                  for r in range(2)]
            hqT = [E2(nc.sbuf_tensor("hqT%d" % r, [128, 512], BF16))
                   for r in range(2)]
            q2b = {k: E2(nc.sbuf_tensor("b2_" + k, [128, 512], F32))
                   for k in ("w", "b", "c2")}
            q2t = [E2(nc.sbuf_tensor("b2_t%d" % r, [128, 512], F32))
                   for r in range(2)]
            q2shs = [E2(nc.sbuf_tensor("b2_shs%d" % r, [128, 512],
                                    BF16 if PORTABLE else F32))
                     for r in range(2)]
            q2s = {k: E2(nc.sbuf_tensor("b2s_" + k, [128, 32], F32))
                   for k in ("bm", "z12", "r2")}
            ps_gate = [E2(nc.psum_tensor("ps_gate%d" % i, [128, 512], F32))
                       for i in range(2)]
            ps_up = [E2(nc.psum_tensor("ps_up%d" % i, [128, 512], F32))
                     for i in range(2)]
            ps_dn = [E2(nc.psum_tensor("ps_dn%d" % i, [128, 512], F32))
                     for i in range(2)]
            ps_htr = [E2(nc.psum_tensor("ps_htr%d" % i, [128, 1024], BF16))
                      for i in range(2)]

            slices = [(n * 512, min(512, I_FFN - n * 512))
                      for n in range((I_FFN + 511) // 512)]
            NS = len(slices)
            bx1 = P.val("s_bx")
            oacc_done = []
            silu_inc = {}
            hmul_inc = {}
            vdone = {}
            absdone = {}
            tr_inc = {}
            evac_inc = {}
            down_inc = {}
            ostg_inc = {}

            pend = [None]

            def flush_pend():
                if pend[0] is None:
                    return
                (pt, psi, pgsi, pp, pnI, pi0) = pend[0]
                pend[0] = None
                P.wait("tensor", "s_bv", vdone[pgsi])
                if pgsi >= 2:
                    P.wait("tensor", "s_bev", evac_inc[pgsi - 2])
                for q_i in range(pnI):
                    P.on("tensor", lambda e, q_i=q_i, p=pp: e.transpose(
                        ps_htr[p][:, q_i * 128:(q_i + 1) * 128],
                        hq[p][:, q_i * 128:(q_i + 1) * 128], identb[:]))
                tr_inc[pgsi] = P.pe_inc("s_bpe")
                P.wait("scalar", "s_bpe", tr_inc[pgsi])
                P.on("scalar", lambda e, nI=pnI, p=pp: e.copy(
                    hqT[p][:, 0:nI * 128], ps_htr[p][:, 0:nI * 128]))
                P.barrier_inc("scalar", "s_bev")
                evac_inc[pgsi] = P.val("s_bev")
                P.wait("tensor", "s_bev", evac_inc[pgsi])
                if psi == 0 and pt >= 1:
                    P.wait("tensor", "s_bv", ostg_inc[pt - 1])
                for half in range(2):
                    for cc in range(pnI):
                        r = pi0 // 128 + cc
                        P.on("tensor", lambda e, h2=half, cc=cc, r=r, p=pp,
                             st=(psi == 0 and cc == 0),
                             sp=(psi == NS - 1 and cc == pnI - 1):
                             e.matmul(
                                 ps_dn[h2][:],
                                 hqT[p][:, cc * 128:(cc + 1) * 128],
                                 wdqT[:, r * D + h2 * 512:
                                      r * D + (h2 + 1) * 512],
                                 start=st, stop=sp))
                down_inc[pgsi] = P.pe_inc("s_bpe")

            def b2_dma(t):
                if t >= NT_OWN:
                    return
                if t >= 2:
                    P.wait("sync", "s_bv", oacc_done[t - 2])
                P.dma("sync", x2_ring[:, (t % 2) * D:(t % 2 + 1) * D],
                      x_ext_d[(HALO + t * 128):(HALO + (t + 1) * 128), :], "s_bx")
                P.dma("sync", attn_stg2[:, (t % 2) * D:(t % 2 + 1) * D],
                      attn_hbm[t * 128:(t + 1) * 128, :], "s_bx")

            gu_inc = {}
            NGSI = NT_OWN * NS

            def emit_gu(g2):
                if g2 >= NGSI:
                    return
                t2, si2 = divmod(g2, NS)
                i0, iw = slices[si2]
                p2 = g2 % 2
                if g2 >= 2:
                    P.wait("tensor", "s_bs", silu_inc[g2 - 2])
                    P.wait("tensor", "s_bv", hmul_inc[g2 - 2])
                for q_i in range(DC):
                    P.on("tensor", lambda e, q_i=q_i, t=t2, i0=i0, iw=iw, p=p2:
                         e.matmul(
                             ps_gate[p][:, 0:iw],
                             xqT[:, (t * DC + q_i) * 128:
                                 (t * DC + q_i + 1) * 128],
                             wgqT[:, q_i * I_FFN + i0:q_i * I_FFN + i0 + iw],
                             start=(q_i == 0), stop=(q_i == DC - 1)))
                for q_i in range(DC):
                    P.on("tensor", lambda e, q_i=q_i, t=t2, i0=i0, iw=iw, p=p2:
                         e.matmul(
                             ps_up[p][:, 0:iw],
                             xqT[:, (t * DC + q_i) * 128:
                                 (t * DC + q_i + 1) * 128],
                             wupqT[:, q_i * I_FFN + i0:q_i * I_FFN + i0 + iw],
                             start=(q_i == 0), stop=(q_i == DC - 1)))
                gu_inc[g2] = P.pe_inc("s_bpe")

            b2_dma(0)
            b2_dma(1)
            emit_gu(0)
            for t in range(NT_OWN):
                P.wait("vector", "s_bx", bx1 + 32 * (t + 1))
                P.on("vector", lambda e, t=t: e.tensor_add(
                    oacc[:],
                    x2_ring[:, (t % 2) * D:(t % 2 + 1) * D],
                    attn_stg2[:, (t % 2) * D:(t % 2 + 1) * D]))
                P.barrier_inc("vector", "s_bv")
                oacc_done.append(P.val("s_bv"))
                b2_dma(t + 2)

                for si, (i0, iw) in enumerate(slices):
                    gsi = t * NS + si
                    p = gsi % 2
                    nI = iw // 128
                    gu_v = gu_inc[gsi]
                    P.wait("scalar", "s_bpe", gu_v)
                    if gsi >= 2:
                        P.wait("scalar", "s_bv", hmul_inc[gsi - 2])
                    P.on("scalar", lambda e, iw=iw, p=p: e.activation(
                        sg[p][:, 0:iw], ps_gate[p][:, 0:iw], AF.Silu))
                    P.barrier_inc("scalar", "s_bs")
                    silu_inc[gsi] = P.val("s_bs")
                    P.wait("vector", "s_bs", silu_inc[gsi])
                    if gsi >= 2:
                        P.wait("vector", "s_bv", vdone[gsi - 2])
                    P.on("vector", lambda e, iw=iw, p=p: e.tensor_mul(
                        hbuf[p][:, 0:iw], sg[p][:, 0:iw], ps_up[p][:, 0:iw]))
                    P.barrier_inc("vector", "s_bv")
                    hmul_inc[gsi] = P.val("s_bv")
                    pvw = []
                    if gsi >= 2:
                        pvw.append(("s_bpe", tr_inc[gsi - 2]))
                    psw = []
                    if gsi >= 2:
                        psw.append(("s_bv", vdone[gsi - 2]))
                    vd, gd = emit_quant(
                        iw, hbuf[p][:, 0:iw],
                        hbuf[p][:, 0:iw].rearrange("p (b i) -> p b i", i=16)
                        .bitcast(I32),
                        hq[p][:, 0:iw], dict(q2b, t=q2t[p], shs=q2shs[p],
                                             **q2s),
                        "s_bv", hmul_inc[gsi], "s_bv", "s_bg",
                        pre_v_waits=pvw, pre_s_waits=psw)
                    vdone[gsi] = vd
                    emit_gu(gsi + 1)
                    flush_pend()
                    pend[0] = (t, si, gsi, p, nI, i0)

                flush_pend()
                P.wait("vector", "s_bpe", down_inc[t * NS + NS - 1])
                if t >= 2:
                    P.wait("vector", "s_out", 16 * (t - 1))
                for half in range(2):
                    P.on("vector", lambda e, h2=half, t=t: e.scalar_tensor_tensor(
                        ostg[:, (t % 2) * D + h2 * 512:
                             (t % 2) * D + (h2 + 1) * 512],
                        ps_dn[h2][:], 1.0,
                        oacc[:, h2 * 512:(h2 + 1) * 512],
                        op0=AL.mult, op1=AL.add))
                P.barrier_inc("vector", "s_bv")
                ostg_inc[t] = P.val("s_bv")
                P.wait("sync", "s_bv", ostg_inc[t])
                P.dma("sync", out_d[t * 128:(t + 1) * 128, :],
                      ostg[:, (t % 2) * D:(t % 2 + 1) * D], "s_out")

            if debug:
                for t in range(NT_OWN):
                    P.dma("sync", dbg_attn[t * 128:(t + 1) * 128, :],
                          attn_hbm[t * 128:(t + 1) * 128, :], "s_out")
                P.dma("sync", dbg_small, dbgS[:], "s_out")
                P.dma("sync", dbg_big, dbgB[:], "s_out")
            P.wait("sync", "s_out", P.val("s_out"))
            P.wait("gpsimd", "s_out", P.val("s_out"))

        with nc.Block() as block:
            @block.sync
            def _(e):
                for fn in P.ops["sync"]:
                    fn(e)

            @block.gpsimd
            def _(e):
                for fn in P.ops["gpsimd"]:
                    fn(e)

            @block.vector
            def _(e):
                for fn in P.ops["vector"]:
                    fn(e)

            @block.scalar
            def _(e):
                for fn in P.ops["scalar"]:
                    fn(e)

            @block.tensor
            def _(e):
                for fn in P.ops["tensor"]:
                    fn(e)

    nc.finalize()
    return nc


def make_in_maps(inputs, n_own):
    x = np.ascontiguousarray(np.asarray(inputs["L0_attn_h"], dtype=np.float32))
    consts_a = make_host_constants(True)
    consts_b = make_host_constants(False)
    wTs = {"w" + w + "T": np.ascontiguousarray(
        np.asarray(inputs["L0_attn_%s_w" % w]).astype(np.float32).T)
        for w in ("q", "k", "v", "pk", "pv")}
    b128 = {}
    for w, nm in (("q", "bq"), ("k", "bk"), ("pk", "bpk")):
        b128[nm] = np.ascontiguousarray(
            np.asarray(inputs["L0_attn_%s_b" % w]).astype(np.float32)
            .reshape(DC, 128).T)
    vb = np.asarray(inputs["L0_attn_v_b"]).astype(np.float32).reshape(1, D)
    pvb = np.asarray(inputs["L0_attn_pv_b"]).astype(np.float32).reshape(1, D)
    wg = np.ascontiguousarray(np.asarray(inputs["L0_ffn_w_gate"], dtype=np.float32))
    wu = np.ascontiguousarray(np.asarray(inputs["L0_ffn_w_up"], dtype=np.float32))
    wd = np.ascontiguousarray(np.asarray(inputs["L0_ffn_w_down"], dtype=np.float32))

    in_maps = []
    n_half = x.shape[1] // 2
    assert n_half == n_own
    for core in range(8):
        b, h = core // 2, core % 2
        if h == 0:
            halo = np.zeros((HALO, D), np.float32)
            xo = x[b, 0:n_half]
        else:
            halo = x[b, n_half - HALO:n_half]
            xo = x[b, n_half:]
        x_ext = np.ascontiguousarray(np.concatenate([halo, xo], axis=0))
        cc = consts_a if h == 0 else consts_b
        m = dict(x_ext=x_ext, vb=vb, pvb=pvb, wg=wg, wu=wu, wd=wd,
                 amask=cc["amask"], maskc=cc["maskc"], ident=cc["ident"],
                 identb=cc["identb"], a4t=cc["a4t"], krow=cc["krow"],
                 ones1=cc["ones1"],
                 epscol=cc["epscol"], rowmask=cc["rowmask"],
                 sgnbit=cc["sgnbit"], absmask=cc["absmask"])
        m.update(wTs)
        m.update(b128)
        in_maps.append(m)
    return in_maps


_NC_CACHE = {}


def kernel(**inputs):
    n_own = S // 2
    if n_own not in _NC_CACHE:
        _NC_CACHE[n_own] = build_nc(n_own)
    nc = _NC_CACHE[n_own]
    in_maps = make_in_maps(inputs, n_own)
    res = run_bass_kernel_spmd(nc, in_maps, core_ids=list(range(8)))
    out = np.empty((B, S, D), np.float32)
    for core in range(8):
        b, h = core // 2, core % 2
        out[b, h * n_own:(h + 1) * n_own] = res.results[core]["out"]
    return out
